# version 08
# baseline (speedup 1.0000x reference)
import numpy as np
import ml_dtypes
import concourse.bass as bass
import concourse.mybir as mybir
from concourse.bass_utils import run_bass_kernel_spmd
from contextlib import ExitStack

F32 = mybir.dt.float32
BF16 = mybir.dt.bfloat16
AF = mybir.ActivationFunctionType
ALU = mybir.AluOpType
AX = mybir.AxisListType
MAGIC = float(1.5 * 2**23)
TWOPI = float(2.0 * np.pi)
INV2PI = float(1.0 / (2.0 * np.pi))
LN5 = float(np.log(5.0))
NCORES = 8
B = 2
S = 4096
D = 1024
NCH = D // 128


ENGINE_ORDER = ("tensor", "vector", "scalar", "gpsimd", "sync")


class _Op:
    __slots__ = ("engine", "fn", "deps", "is_dma", "idx", "sem_idx", "needs_inc", "tag")

    def __init__(self, engine, fn, is_dma, tag):
        self.engine = engine
        self.fn = fn
        self.deps = []
        self.is_dma = is_dma
        self.idx = None
        self.sem_idx = None
        self.needs_inc = False
        self.tag = tag


class Buf:
    __slots__ = ("t", "writers", "readers", "name")

    def __init__(self, t, name=""):
        self.t = t
        self.writers = []
        self.readers = []
        self.name = name

    def __getitem__(self, key):
        return self.t[key]

    @property
    def ap(self):
        return self.t[:]


class Sched:
    def __init__(self, nc, st):
        self.nc = nc
        self.st = st
        self.ops = []
        self.sems = {}
        self.counts = {}
        self.high = {}
        self.barrier_deps = []
        self.last_op_per_stream = {}

    def op(self, engine, fn, reads=(), writes=(), is_dma=False, tag=""):
        o = _Op(engine, fn, is_dma, tag)
        o.idx = len(self.ops)
        o.deps.extend(self.barrier_deps)
        for b in reads:
            o.deps.extend(b.writers)
            b.readers.append(o)
        for b in writes:
            o.deps.extend(b.readers)
            o.deps.extend(b.writers)
            b.writers = [o]
            b.readers = []
        self.ops.append(o)
        self.last_op_per_stream[(engine, is_dma)] = o
        return o

    def barrier(self):
        self.barrier_deps = list(self.last_op_per_stream.values())

    def _sem_key(self, o):
        return ("dma_" + o.engine) if o.is_dma else ("c_" + o.engine)

    def flush(self):
        nc = self.nc
        ops = self.ops
        if not ops:
            return
        for o in ops:
            for dep in o.deps:
                if dep.is_dma or o.is_dma or dep.engine != o.engine:
                    dep.needs_inc = True
        for o in ops:
            if o.needs_inc or o.is_dma:
                key = self._sem_key(o)
                if key not in self.sems:
                    self.sems[key] = self.st.enter_context(self.nc.semaphore(key))
                    self.counts[key] = 0
                self.counts[key] += 16 if o.is_dma else 1
                o.sem_idx = (key, self.counts[key])
        engines_used = sorted({o.engine for o in ops},
                              key=lambda e: ENGINE_ORDER.index(e))
        with nc.Block() as block:
            for eng_name in engines_used:
                eng_ops = [o for o in ops if o.engine == eng_name]
                high = self.high.setdefault(eng_name, {})

                def body(eng, eng_ops=eng_ops, eng_name=eng_name, high=high):
                    for o in eng_ops:
                        for dep in o.deps:
                            if dep.sem_idx is None:
                                continue
                            if ((not dep.is_dma) and (not o.is_dma)
                                    and dep.engine == eng_name):
                                continue
                            key, val = dep.sem_idx
                            if high.get(key, 0) >= val:
                                continue
                            eng.wait_ge(self.sems[key], val)
                            high[key] = val
                        inst = o.fn(eng)
                        if o.sem_idx is not None:
                            key, _val = o.sem_idx
                            if inst is None:
                                raise RuntimeError(f"op {o.tag} needs inc, emitted nothing")
                            if o.is_dma or o.engine in ("tensor", "sync"):
                                inst.then_inc(self.sems[key], 16 if o.is_dma else 1)
                            else:
                                eng.drain().then_inc(self.sems[key], 1)

                getattr(block, eng_name)(body)
        self.ops = []

    def finish(self):
        self.flush()
        nc = self.nc
        dma_keys = [k for k in self.counts if k.startswith("dma_")]
        with nc.Block() as block:
            def tail(eng):
                for k in dma_keys:
                    eng.wait_ge(self.sems[k], self.counts[k])
            block.sync(tail)


class Prog:
    def __init__(self):
        self.nc = bass.Bass("TRN2", target_bir_lowering=False, debug=False)
        self.st = ExitStack()
        self.sched = Sched(self.nc, self.st)
        self.n = 0
        self.prefix = ""

    def sb(self, shape, dtype, name=None, stack=None):
        self.n += 1
        nm = self.prefix + (name or f"sb{self.n}")
        t = (stack or self.st).enter_context(
            self.nc.sbuf_tensor(nm, shape, dtype))
        return Buf(t, nm)

    def ps(self, shape, name=None, stack=None):
        self.n += 1
        nm = self.prefix + (name or f"ps{self.n}")
        t = (stack or self.st).enter_context(
            self.nc.psum_tensor(nm, shape, F32))
        return Buf(t, nm)

    def dram_in(self, name, shape, dtype=F32):
        return self.nc.dram_tensor(name, shape, dtype, kind="ExternalInput").ap()

    def dram_out(self, name, shape, dtype=F32):
        return self.nc.dram_tensor(name, shape, dtype, kind="ExternalOutput").ap()

    def dram_tmp(self, name, shape, dtype=F32):
        return self.nc.dram_tensor(name, shape, dtype, kind="Internal").ap()

    def finish(self):
        self.sched.finish()
        self.st.close()
        return self.nc



def t3ap(buf, nch, T, p=128):
    W = nch * T
    return bass.AP(buf.t, 0, [[W, p], [T, nch], [1, T]])


def load_tile(p, X, dram_slice, nch, T, queue="sync", tag="ldx", hbuf=None):
    src = dram_slice.rearrange("(c p) t -> p c t", p=128)
    return p.sched.op(queue,
                      lambda e, src=src: e.dma_start(out=t3ap(X, nch, T), in_=src),
                      writes=[X], reads=([hbuf] if hbuf is not None else []),
                      is_dma=True, tag=tag)


def store_tile(p, dram_slice, X, nch, T, queue="scalar", tag="stx", hbuf=None):
    dst = dram_slice.rearrange("(c p) t -> p c t", p=128)
    return p.sched.op(queue,
                      lambda e, dst=dst: e.dma_start(out=dst, in_=t3ap(X, nch, T)),
                      reads=[X], writes=([hbuf] if hbuf is not None else []),
                      is_dma=True, tag=tag)


def load_wT(p, wT_ap, K, O, name, stack=None):
    nk = K // 128
    wb = p.sb([128, nk * O], BF16, name, stack=stack)
    for kc in range(nk):
        p.sched.op("gpsimd",
                   lambda e, kc=kc: e.dma_start(
                       out=wb[:, kc * O:(kc + 1) * O],
                       in_=wT_ap[kc * 128:(kc + 1) * 128, :]),
                   writes=[wb], is_dma=True, tag=f"{name}_ld{kc}")
    return wb


def load_cols(p, ap, n, name, dtype=F32, stack=None):
    cb = p.sb([128, n], dtype, name, stack=stack)
    p.sched.op("sync", lambda e: e.dma_start(out=cb[:], in_=ap[:]),
               writes=[cb], is_dma=True, tag=f"{name}_ld")
    return cb


class Consts:

    def __init__(self, p, T):
        self.T = T
        sch = p.sched
        self.ones_col_bf = p.sb([128, 1], BF16, "ones_col_bf")
        self.ones_row_f32 = p.sb([1, 128], F32, "ones_row_f32")
        self.ones_row_bf = p.sb([1, max(T, 512)], BF16, "ones_row_bf")
        self.eps5 = p.sb([128, 1], F32, "eps5")
        self.eps8 = p.sb([128, 1], F32, "eps8")
        self.ln5 = p.sb([128, 1], F32, "ln5")
        for b_, v in ((self.ones_col_bf, 1.0), (self.ones_row_f32, 1.0),
                      (self.ones_row_bf, 1.0), (self.eps5, 1e-5),
                      (self.eps8, 1e-8), (self.ln5, LN5)):
            sch.op("vector", lambda e, b_=b_, v=v: e.memset(b_.ap, v), writes=[b_],
                   tag="const")


def make_rows(p, T, stack=None):
    rows = {k: p.sb([1, T], F32, None, stack=stack)
            for k in ("mu", "var", "rstd", "negmurstd")}
    rows["negmurstd_bf"] = p.sb([1, T], BF16, None, stack=stack)
    return rows


def emit_ln_prep(p, X, T, nch, xb, sq, tag="lnp"):
    sch = p.sched
    W = nch * T
    sch.op("vector", lambda e: e.tensor_copy(xb[:, 0:W], X[:, 0:W]),
           reads=[X], writes=[xb], tag=f"{tag}_cast")
    sch.op("scalar", lambda e: e.activation(sq[:, 0:W], X[:, 0:W], AF.Square),
           reads=[X], writes=[sq], tag=f"{tag}_sq")


def emit_ln_main(p, C, T, nch, xb, sq, psA, psB, bc_ps, rows, tag="ln"):
    sch = p.sched
    ND = nch * 128
    for c in range(nch):
        sch.op("tensor", lambda e, c=c: e.matmul(
            psA[0:1, 0:T], C.ones_col_bf[:, 0:1], xb[:, c * T:(c + 1) * T],
            start=(c == 0), stop=(c == nch - 1)),
            reads=[xb, C.ones_col_bf], writes=[psA], tag=f"{tag}_mA{c}")
    for c in range(nch):
        sch.op("tensor", lambda e, c=c: e.matmul(
            psB[0:1, 0:T], C.ones_col_bf[:, 0:1], sq[:, c * T:(c + 1) * T],
            start=(c == 0), stop=(c == nch - 1)),
            reads=[sq, C.ones_col_bf], writes=[psB], tag=f"{tag}_mB{c}")
    sch.op("vector", lambda e: e.tensor_scalar_mul(rows["mu"][0:1, 0:T],
                                                   psA[0:1, 0:T], 1.0 / ND),
           reads=[psA], writes=[rows["mu"]], tag=f"{tag}_mu")
    sch.op("vector", lambda e: e.scalar_tensor_tensor(
        out=rows["var"][0:1, 0:T], in0=rows["mu"][0:1, 0:T], scalar=-1.0,
        in1=rows["mu"][0:1, 0:T], op0=ALU.mult, op1=ALU.mult),
        reads=[rows["mu"]], writes=[rows["var"]], tag=f"{tag}_m2")
    sch.op("vector", lambda e: e.scalar_tensor_tensor(
        out=rows["var"][0:1, 0:T], in0=psB[0:1, 0:T], scalar=1.0 / ND,
        in1=rows["var"][0:1, 0:T], op0=ALU.mult, op1=ALU.add),
        reads=[psB, rows["var"]], writes=[rows["var"]], tag=f"{tag}_var")
    sch.op("scalar", lambda e: e.activation(rows["rstd"][0:1, 0:T],
                                            rows["var"][0:1, 0:T], AF.Ln,
                                            bias=C.eps5[0:1, 0:1]),
           reads=[rows["var"], C.eps5], writes=[rows["rstd"]], tag=f"{tag}_l")
    sch.op("scalar", lambda e: e.activation(rows["rstd"][0:1, 0:T],
                                            rows["rstd"][0:1, 0:T], AF.Exp,
                                            scale=-0.5),
           reads=[rows["rstd"]], writes=[rows["rstd"]], tag=f"{tag}_e")
    sch.op("vector", lambda e: e.scalar_tensor_tensor(
        out=rows["negmurstd"][0:1, 0:T], in0=rows["mu"][0:1, 0:T], scalar=-1.0,
        in1=rows["rstd"][0:1, 0:T], op0=ALU.mult, op1=ALU.mult),
        reads=[rows["mu"], rows["rstd"]], writes=[rows["negmurstd"]], tag=f"{tag}_n")
    sch.op("vector", lambda e: e.tensor_copy(rows["negmurstd_bf"][0:1, 0:T],
                                             rows["negmurstd"][0:1, 0:T]),
           reads=[rows["negmurstd"]], writes=[rows["negmurstd_bf"]], tag=f"{tag}_nb")
    sch.op("tensor", lambda e: e.matmul(bc_ps[:, 0:T], C.ones_row_f32[0:1, :],
                                        rows["rstd"][0:1, 0:T], start=True, stop=True),
           reads=[rows["rstd"], C.ones_row_f32], writes=[bc_ps], tag=f"{tag}_bc")


def emit_ln(p, C, X, T, nch, xb, sq, psA, psB, bc_ps, rows, tag="ln"):
    emit_ln_prep(p, X, T, nch, xb, sq, tag=tag + "p")
    emit_ln_main(p, C, T, nch, xb, sq, psA, psB, bc_ps, rows, tag=tag)


def emit_xtilde(p, X, bc_ps, out_buf, T, nch, tag="xt"):
    for c in range(nch):
        p.sched.op("vector", lambda e, c=c: e.tensor_mul(
            out_buf[:, c * T:(c + 1) * T], X[:, c * T:(c + 1) * T], bc_ps[:, 0:T]),
            reads=[X, bc_ps], writes=[out_buf], tag=f"{tag}{c}")


def emit_mm_group(p, wT_sb, act_bf, T, nk, O_total, psums, evac_fn,
                  rank1s=(), tag="mm"):
    sch = p.sched
    noo = O_total // 128
    n_extra = len(rank1s)
    for oo in range(noo):
        psum = psums[oo % len(psums)]
        for kc in range(nk):
            sch.op("tensor", lambda e, kc=kc, oo=oo, psum=psum: e.matmul(
                psum[:, 0:T],
                wT_sb[:, kc * O_total + oo * 128: kc * O_total + oo * 128 + 128],
                act_bf[:, kc * T:(kc + 1) * T],
                start=(kc == 0), stop=(kc == nk - 1 and n_extra == 0)),
                reads=[wT_sb, act_bf], writes=[psum], tag=f"{tag}o{oo}k{kc}")
        for j, (lhs_fn, rhs_ap, rbufs) in enumerate(rank1s):
            sch.op("tensor", lambda e, oo=oo, psum=psum, lhs_fn=lhs_fn, rhs_ap=rhs_ap, j=j:
                   e.matmul(psum[:, 0:T], lhs_fn(oo), rhs_ap,
                            start=False, stop=(j == n_extra - 1)),
                   reads=list(rbufs), writes=[psum], tag=f"{tag}o{oo}r{j}")
        evac_fn(oo, psum)


def emit_trig(p, U, W, kscr, sin_out, cos_out, tag="tr"):
    sch = p.sched
    sch.op("vector", lambda e: e.tensor_scalar(
        out=kscr[:, 0:W], in0=U[:, 0:W], scalar1=MAGIC, scalar2=MAGIC,
        op0=ALU.add, op1=ALU.subtract),
        reads=[U], writes=[kscr], tag=f"{tag}_k")
    sch.op("vector", lambda e: e.tensor_sub(sin_out[:, 0:W], U[:, 0:W], kscr[:, 0:W]),
           reads=[U, kscr], writes=[sin_out], tag=f"{tag}_r")
    sch.op("vector", lambda e: e.tensor_scalar(
        out=kscr[:, 0:W], in0=sin_out[:, 0:W], scalar1=0.25, scalar2=MAGIC,
        op0=ALU.add, op1=ALU.add),
        reads=[sin_out], writes=[kscr], tag=f"{tag}_r3")
    sch.op("vector", lambda e: e.tensor_scalar(
        out=kscr[:, 0:W], in0=kscr[:, 0:W], scalar1=MAGIC, scalar2=0.25,
        op0=ALU.subtract, op1=ALU.subtract),
        reads=[kscr], writes=[kscr], tag=f"{tag}_kq")
    sch.op("vector", lambda e: e.tensor_sub(cos_out[:, 0:W], sin_out[:, 0:W],
                                            kscr[:, 0:W]),
           reads=[sin_out, kscr], writes=[cos_out], tag=f"{tag}_r2")
    sch.op("scalar", lambda e: e.activation(sin_out[:, 0:W], sin_out[:, 0:W],
                                            AF.Sin, scale=TWOPI),
           reads=[sin_out], writes=[sin_out], tag=f"{tag}_s")
    sch.op("scalar", lambda e: e.activation(cos_out[:, 0:W], cos_out[:, 0:W],
                                            AF.Sin, scale=TWOPI),
           reads=[cos_out], writes=[cos_out], tag=f"{tag}_c")


def emit_wsum(p, C, wb, nk, O_total, name, psw, stack=None):
    sch = p.sched
    row_bf = p.sb([1, O_total], BF16, name + "_bf", stack=stack)
    for og in range((O_total + 511) // 512):
        o0, o1 = og * 512, min((og + 1) * 512, O_total)
        for kc in range(nk):
            sch.op("tensor", lambda e, kc=kc, o0=o0, o1=o1: e.matmul(
                psw[0:1, 0:o1 - o0], C.ones_col_bf[:, 0:1],
                wb[:, kc * O_total + o0: kc * O_total + o1],
                start=(kc == 0), stop=(kc == nk - 1)),
                reads=[wb, C.ones_col_bf], writes=[psw], tag=f"{name}{og}_{kc}")
        sch.op("vector", lambda e, o0=o0, o1=o1: e.tensor_copy(
            row_bf[0:1, o0:o1], psw[0:1, 0:o1 - o0]),
            reads=[psw], writes=[row_bf], tag=f"{name}cp{og}")
    return row_bf



NAMES_L0A = ["L0_attn_W_magT", "L0_attn_b_mag_c", "L0_attn_W_qT",
             "L0_attn_b_q_c", "L0_attn_random_phases_r", "pos_row",
             "L0_attn_W1T", "L0_attn_b1_c", "L0_attn_W2T", "L0_attn_b2_c",
             "L0_attn_ln_g_c", "L0_attn_ln_b_c"]


def emit_l0_attn(p, C, io, T=256, dbg=False):
    sch = p.sched
    p.prefix = "A_"
    xin_b, xout_b = io["xin"], io["xout"]
    x_h, xout_h = xin_b.t, xout_b.t
    wmagT_h = io["L0_attn_W_magT"]
    bmag_h = io["L0_attn_b_mag_c"]
    wqT_h = io["L0_attn_W_qT"]
    bq_h = io["L0_attn_b_q_c"]
    rp_h = io["L0_attn_random_phases_r"]
    pos_h = io["pos_row"]
    w1T_h = io["L0_attn_W1T"]
    b1_h = io["L0_attn_b1_c"]
    w2T_h = io["L0_attn_W2T"]
    b2_h = io["L0_attn_b2_c"]
    lng_h = io["L0_attn_ln_g_c"]
    lnb_h = io["L0_attn_ln_b_c"]
    mk = p.dram_out if dbg else p.dram_tmp
    ctx_h = mk("ctx", [B, 4 * D, S], BF16)
    hbuf_h = mk("l0h", [B, 2 * D, S], BF16)
    ctx_b = Buf(ctx_h, "ctx")
    hb_b = Buf(hbuf_h, "l0h")
    dbgw_h = p.dram_out("dbgw", [128, 32 * D], BF16) if dbg else None
    dbgb_h = p.dram_out("dbgb", [128, 16], F32) if dbg else None

    W = NCH * T
    NPT = S // T

    T1 = 256
    W1_ = NCH * T1
    NPT1 = S // T1
    ph = ExitStack()
    wmag = load_wT(p, wmagT_h, D, D, "wmag", stack=ph)
    wq = load_wT(p, wqT_h, D, D, "wq", stack=ph)
    bmag = load_cols(p, bmag_h, NCH, "bmag", stack=ph)
    bq = load_cols(p, bq_h, NCH, "bq", stack=ph)
    rp = p.sb([1, D], F32, "rp", stack=ph)
    sch.op("sync", lambda e: e.dma_start(out=rp[:], in_=rp_h[:]), writes=[rp],
           is_dma=True, tag="rp_ld")
    _posT = p.sb([1, 512], F32, "posT", stack=ph)
    posTs = [_posT, _posT]
    rps = rp
    sch.op("vector", lambda e: e.tensor_scalar_mul(rp[0:1, :], rp[0:1, :],
                                                   float(0.01 * INV2PI)),
           reads=[rp], writes=[rp], tag="rps")
    bq2 = p.sb([128, NCH], F32, "bq2", stack=ph)
    sch.op("vector", lambda e: e.tensor_scalar_mul(bq2[:], bq[:], INV2PI),
           reads=[bq], writes=[bq2], tag="bq2")

    psA = p.ps([128, 512], "psA", stack=ph)
    psB = p.ps([128, 512], "psB", stack=ph)
    ps_bc = p.ps([128, 512], "ps_bc", stack=ph)
    ps_m0 = p.ps([128, 512], "ps_m0", stack=ph)
    ps_m1 = p.ps([128, 512], "ps_m1", stack=ph)
    ps_u = p.ps([128, 512], "ps_u", stack=ph)
    ps_u2 = p.ps([128, 512], "ps_u2", stack=ph)

    wmagsum_bf = emit_wsum(p, C, wmag, NCH, D, "wmagsum", ps_m0, stack=ph)
    wqsum_bf = emit_wsum(p, C, wq, NCH, D, "wqsum", ps_m0, stack=ph)

    Xs = [p.sb([128, W1_], F32, f"pX{j}", stack=ph) for j in range(2)]
    XBs = [p.sb([128, W1_], BF16, f"pXB{j}", stack=ph) for j in range(2)]
    SQBs = [p.sb([128, W1_], BF16, f"pSQB{j}", stack=ph) for j in range(2)]
    UFMs = [p.sb([128, W1_], F32, f"pUFM{j}", stack=ph) for j in range(2)]
    COSPs = [p.sb([128, W1_], F32, f"pCOS{j}", stack=ph) for j in range(2)]
    SINPs = [p.sb([128, W1_], F32, f"pSIN{j}", stack=ph) for j in range(2)]
    KSCR1 = p.sb([128, W1_], F32, "pKS", stack=ph)
    XLN = p.sb([128, W1_], F32, "pXLN", stack=ph)
    MAGT = p.sb([128, W1_], F32, "pMAGT", stack=ph)
    CQ = p.sb([128, W1_], F32, "pCQ", stack=ph)
    MEMR = p.sb([128, W1_], F32, "pMEMR", stack=ph)
    MEMI = p.sb([128, W1_], F32, "pMEMI", stack=ph)
    ACC = p.sb([128, W1_], F32, "pACC", stack=ph)
    SCR1 = p.sb([128, W1_], F32, "pSCR1", stack=ph)
    SCR2 = p.sb([128, W1_], F32, "pSCR2", stack=ph)
    WGT = ACC
    RAWS = p.sb([128, 512], F32, "pRAWS", stack=ph)
    QOUT = p.sb([128, W1_], BF16, "pQOUT", stack=ph)
    rows = make_rows(p, T1, stack=ph)
    carries = {s_: p.sb([128, NCH * B], F32, "carry_" + s_, stack=ph)
               for s_ in ("r", "i", "a")}
    for s_ in carries:
        sch.op("vector", lambda e, s_=s_: e.memset(carries[s_].ap, 0.0),
               writes=[carries[s_]], tag="car0")

    mu_r1 = [(lambda oo, w=wmagsum_bf: w[0:1, oo * 128:(oo + 1) * 128],
              rows["negmurstd_bf"][0:1, 0:T1], [rows["negmurstd_bf"]])]
    mu_r1q = [(lambda oo, w=wqsum_bf: w[0:1, oo * 128:(oo + 1) * 128],
               rows["negmurstd_bf"][0:1, 0:T1], [rows["negmurstd_bf"]])]

    tiles1 = [(pt, b) for pt in range(NPT1) for b in range(B)]

    def prep1(i):
        pt, b = tiles1[i]
        t0 = pt * T1
        X = Xs[i % 2]
        load_tile(p, X, x_h[b, :, t0:t0 + T1], NCH, T1, tag=f"px{i}",
                  hbuf=xin_b)
        emit_ln_prep(p, X, T1, NCH, XBs[i % 2], SQBs[i % 2], tag=f"plp{i}")
        if b == 0:
            j = pt % 2
            posT, UFM, COSP, SINP, KSCR = (posTs[j], UFMs[j], COSPs[j],
                                           SINPs[j], KSCR1)
            sch.op("sync", lambda e, t0=t0, posT=posT: e.dma_start(
                out=posT[0:1, 0:T1], in_=pos_h[0:1, t0:t0 + T1]),
                writes=[posT], is_dma=True, tag=f"pos{pt}")
            for c in range(NCH):
                sch.op("tensor", lambda e, c=c, posT=posT: e.matmul(
                    ps_u2[:, 0:T1], rps[0:1, c * 128:(c + 1) * 128],
                    posT[0:1, 0:T1], start=True, stop=True),
                    reads=[rps, posT], writes=[ps_u2], tag=f"phi{pt}_{c}")
                sch.op("vector", lambda e, c=c, UFM=UFM: e.tensor_copy(
                    UFM[:, c * T1:(c + 1) * T1], ps_u2[:, 0:T1]),
                    reads=[ps_u2], writes=[UFM], tag=f"phic{pt}_{c}")
            emit_trig(p, UFM, W1_, KSCR, SINP, COSP, tag=f"tr{pt}")

    def main1(i):
        pt, b = tiles1[i]
        t0 = pt * T1
        j = pt % 2
        X, XB, SQB = Xs[i % 2], XBs[i % 2], SQBs[i % 2]
        UFM, COSP, SINP = UFMs[j], COSPs[j], SINPs[j]
        QOUT2 = SQB
        SQT = KSCR1
        UQ = SCR1
        emit_ln_main(p, C, T1, NCH, XB, SQB, psA, psB, ps_bc, rows,
                     tag=f"pln{i}")
        emit_xtilde(p, X, ps_bc, XB, T1, NCH, tag=f"pxt{i}")
        sch.op("tensor", lambda e: e.matmul(
            ps_u[:, 0:T1], C.ones_row_f32[0:1, :], rows["negmurstd"][0:1, 0:T1],
            start=True, stop=True),
            reads=[rows["negmurstd"], C.ones_row_f32], writes=[ps_u],
            tag=f"pnmb{i}")
        emit_xtilde(p, X, ps_bc, XLN, T1, NCH, tag=f"pxl{i}")
        for c in range(NCH):
            sch.op("vector", lambda e, c=c: e.tensor_add(
                XLN[:, c * T1:(c + 1) * T1], XLN[:, c * T1:(c + 1) * T1],
                ps_u[:, 0:T1]),
                reads=[XLN, ps_u], writes=[XLN], tag=f"pxa{i}{c}")

        def evac_mag(oo, psum):
            sch.op("scalar", lambda e, oo=oo, psum=psum: e.activation(
                MAGT[:, oo * T1:(oo + 1) * T1], psum[:, 0:T1], AF.Sigmoid,
                bias=bmag[:, oo:oo + 1]),
                reads=[psum, bmag], writes=[MAGT], tag=f"pmv{i}{oo}")
        emit_mm_group(p, wmag, XB, T1, NCH, D, (ps_m0, ps_m1), evac_mag,
                      rank1s=mu_r1, tag=f"pm{i}")

        def evac_q(oo, psum):
            sch.op("scalar", lambda e, oo=oo, psum=psum: e.activation(
                UQ[:, oo * T1:(oo + 1) * T1], psum[:, 0:T1], AF.Identity,
                scale=INV2PI, bias=bq2[:, oo:oo + 1]),
                reads=[psum, bq2], writes=[UQ], tag=f"pqv{i}{oo}")
        emit_mm_group(p, wq, XB, T1, NCH, D, (ps_m0, ps_m1), evac_q,
                      rank1s=mu_r1q, tag=f"pq{i}")
        sch.op("vector", lambda e, UFM=UFM: e.tensor_add(
            UQ[:, 0:W1_], UQ[:, 0:W1_], UFM[:, 0:W1_]),
            reads=[UQ, UFM], writes=[UQ], tag=f"puq{i}")
        emit_trig(p, UQ, W1_, SCR2, SQT, CQ, tag=f"ptq{i}")

        sch.op("gpsimd", lambda e: e.tensor_mul(WGT[:, 0:W1_], MAGT[:, 0:W1_],
                                                XLN[:, 0:W1_]),
               reads=[MAGT, XLN], writes=[WGT], tag=f"pwg{i}")
        sch.op("gpsimd", lambda e, COSP=COSP: e.tensor_mul(
            SCR2[:, 0:W1_], WGT[:, 0:W1_], COSP[:, 0:W1_]),
            reads=[WGT, COSP], writes=[SCR2], tag=f"pwc{i}")
        sch.op("gpsimd", lambda e, SINP=SINP: e.tensor_mul(
            SCR1[:, 0:W1_], WGT[:, 0:W1_], SINP[:, 0:W1_]),
            reads=[WGT, SINP], writes=[SCR1], tag=f"pws{i}")

        def scans(srcb, dst, key):
            car = carries[key]
            for c in range(NCH):
                cc_ = np.s_[:, c * T1:(c + 1) * T1]
                col = np.s_[:, b * NCH + c: b * NCH + c + 1]
                sch.op("vector", lambda e, cc_=cc_, srcb=srcb: e.tensor_tensor_scan(
                    out=RAWS[:, 0:T1], data0=srcb[cc_], data1=srcb[cc_],
                    initial=0.0, op0=ALU.add, op1=ALU.bypass),
                    reads=[srcb], writes=[RAWS], tag=f"psc{key}{i}{c}")
                sch.op("vector", lambda e, cc_=cc_, col=col, dst=dst: e.tensor_scalar_add(
                    dst[cc_], RAWS[:, 0:T1], car[col]),
                    reads=[RAWS, car], writes=[dst], tag=f"pcb{key}{i}{c}")
                sch.op("vector", lambda e, col=col: e.tensor_add(
                    car[col], car[col], RAWS[:, T1 - 1:T1]),
                    reads=[car, RAWS], writes=[car], tag=f"pcu{key}{i}{c}")
        scans(SCR2, MEMR, "r")
        scans(SCR1, MEMI, "i")
        scans(MAGT, ACC, "a")

        sch.op("scalar", lambda e: e.activation(ACC[:, 0:W1_], ACC[:, 0:W1_],
                                                AF.Ln, scale=5.0,
                                                bias=C.eps8[:, 0:1]),
               reads=[ACC, C.eps8], writes=[ACC], tag=f"pisl{i}")
        sch.op("scalar", lambda e: e.activation(ACC[:, 0:W1_], ACC[:, 0:W1_],
                                                AF.Exp, scale=-0.5,
                                                bias=C.ln5[:, 0:1]),
               reads=[ACC, C.ln5], writes=[ACC], tag=f"pise{i}")
        sch.op("vector", lambda e: e.tensor_mul(MEMR[:, 0:W1_], MEMR[:, 0:W1_],
                                                ACC[:, 0:W1_]),
               reads=[MEMR, ACC], writes=[MEMR], tag=f"pmr{i}")
        sch.op("vector", lambda e: e.tensor_mul(MEMI[:, 0:W1_], MEMI[:, 0:W1_],
                                                ACC[:, 0:W1_]),
               reads=[MEMI, ACC], writes=[MEMI], tag=f"pmi{i}")

        def quarter_store(q, buf, tag):
            src3 = bass.AP(buf.t, 0, [[W1_, 128], [T1, NCH], [1, T1]])
            dst = ctx_h[b, q * D:(q + 1) * D, t0:t0 + T1].rearrange(
                "(c p) t -> p c t", p=128)
            sch.op("scalar", lambda e, dst=dst, src3=src3: e.dma_start(
                out=dst, in_=src3),
                reads=[buf], writes=[ctx_b], is_dma=True, tag=tag)

        sch.op("gpsimd", lambda e: e.tensor_mul(SCR1[:, 0:W1_], MEMR[:, 0:W1_],
                                                CQ[:, 0:W1_]),
               reads=[MEMR, CQ], writes=[SCR1], tag=f"pt1{i}")
        sch.op("vector", lambda e: e.tensor_mul(SCR2[:, 0:W1_], MEMI[:, 0:W1_],
                                                SQT[:, 0:W1_]),
               reads=[MEMI, SQT], writes=[SCR2], tag=f"pt2{i}")
        sch.op("vector", lambda e: e.tensor_add(QOUT[:, 0:W1_], SCR1[:, 0:W1_],
                                                SCR2[:, 0:W1_]),
               reads=[SCR1, SCR2], writes=[QOUT], tag=f"prr{i}")
        quarter_store(2, QOUT, f"pstr{i}")
        sch.op("gpsimd", lambda e: e.tensor_mul(SCR1[:, 0:W1_], MEMI[:, 0:W1_],
                                                CQ[:, 0:W1_]),
               reads=[MEMI, CQ], writes=[SCR1], tag=f"pt3{i}")
        sch.op("vector", lambda e: e.tensor_mul(SCR2[:, 0:W1_], MEMR[:, 0:W1_],
                                                SQT[:, 0:W1_]),
               reads=[MEMR, SQT], writes=[SCR2], tag=f"pt4{i}")
        sch.op("vector", lambda e: e.tensor_sub(QOUT2[:, 0:W1_], SCR1[:, 0:W1_],
                                                SCR2[:, 0:W1_]),
               reads=[SCR1, SCR2], writes=[QOUT2], tag=f"pri{i}")
        quarter_store(3, QOUT2, f"psti{i}")
        sch.op("gpsimd", lambda e, COSP=COSP: e.tensor_mul(
            QOUT[:, 0:W1_], XLN[:, 0:W1_], COSP[:, 0:W1_]),
            reads=[XLN, COSP], writes=[QOUT], tag=f"pq1{i}")
        quarter_store(0, QOUT, f"pst1{i}")
        sch.op("gpsimd", lambda e, SINP=SINP: e.tensor_mul(
            QOUT2[:, 0:W1_], XLN[:, 0:W1_], SINP[:, 0:W1_]),
            reads=[XLN, SINP], writes=[QOUT2], tag=f"pq2{i}")
        quarter_store(1, QOUT2, f"pst2{i}")

    prep1(0)
    for i in range(len(tiles1)):
        if i + 1 < len(tiles1):
            prep1(i + 1)
        main1(i)

    sch.flush()
    ph.close()
    sch.barrier()

    NF = 32
    TA = 256
    TT = 2 * TA
    NPTA = S // TA
    for half in range(2):
        ph2 = ExitStack()
        OH = D
        w1 = p.sb([128, NF * OH], BF16, f"w1h{half}", stack=ph2)
        for fc in range(NF):
            sch.op("gpsimd", lambda e, fc=fc: e.dma_start(
                out=w1[:, fc * OH:(fc + 1) * OH],
                in_=w1T_h[fc * 128:(fc + 1) * 128, half * OH:(half + 1) * OH]),
                writes=[w1], is_dma=True, tag=f"w1l{half}_{fc}")
        b1c = load_cols(p, b1_h, 16, f"b1c{half}", stack=ph2)
        lng = load_cols(p, lng_h, NF, f"lng{half}", stack=ph2)
        lnb = load_cols(p, lnb_h, NF, f"lnb{half}", stack=ph2)
        lnb_bf = p.sb([128, NF], BF16, f"lnbbf{half}", stack=ph2)
        sch.op("vector", lambda e: e.tensor_copy(lnb_bf[:], lnb[:]),
               reads=[lnb], writes=[lnb_bf], tag=f"lnbb{half}")
        psA2 = p.ps([128, 512], f"psA2{half}", stack=ph2)
        psB2 = p.ps([128, 512], f"psB2{half}", stack=ph2)
        ps_bc2 = p.ps([128, 512], f"psbc2{half}", stack=ph2)
        ps_n0 = p.ps([128, 512], f"psn0{half}", stack=ph2)
        ps_n1 = p.ps([128, 512], f"psn1{half}", stack=ph2)
        ps_u2 = p.ps([128, 512], f"psu2{half}", stack=ph2)

        w1b_row = p.sb([1, OH], F32, f"w1br{half}", stack=ph2)
        for og in range(OH // 512):
            for fc in range(NF):
                sch.op("tensor", lambda e, og=og, fc=fc: e.matmul(
                    ps_n0[0:1, 0:512], lnb_bf[:, fc:fc + 1],
                    w1[:, fc * OH + og * 512: fc * OH + (og + 1) * 512],
                    start=(fc == 0), stop=(fc == NF - 1)),
                    reads=[w1, lnb_bf], writes=[ps_n0], tag=f"w1b{half}{og}_{fc}")
            sch.op("vector", lambda e, og=og: e.tensor_copy(
                w1b_row[0:1, og * 512:(og + 1) * 512], ps_n0[0:1, 0:512]),
                reads=[ps_n0], writes=[w1b_row], tag=f"w1bc{half}{og}")
        biasH = p.sb([128, 8], F32, f"biasH{half}", stack=ph2)
        for j in range(8):
            sch.op("tensor", lambda e, j=j: e.matmul(
                ps_u2[:, j:j + 1], w1b_row[0:1, j * 128:(j + 1) * 128],
                C.ones_row_f32[0:1, 0:1], start=True, stop=True),
                reads=[w1b_row, C.ones_row_f32], writes=[ps_u2], tag=f"w1bt{half}{j}")
        sch.op("vector", lambda e, half=half: e.tensor_add(
            biasH[:], ps_u2[:, 0:8], b1c[:, half * 8:(half + 1) * 8]),
            reads=[ps_u2, b1c], writes=[biasH], tag=f"biasH{half}")

        for fc in range(NF):
            sch.op("scalar", lambda e, fc=fc: e.activation(
                w1[:, fc * OH:(fc + 1) * OH], w1[:, fc * OH:(fc + 1) * OH],
                AF.Identity, scale=lng[:, fc:fc + 1]),
                reads=[w1, lng], writes=[w1], tag=f"w1g{half}{fc}")
        w1gsum_bf = emit_wsum(p, C, w1, NF, OH, f"w1gs{half}", ps_n0, stack=ph2)

        CTXs = [p.sb([128, NF * TT], BF16, f"CTX{half}_{j}", stack=ph2)
                for j in range(2)]
        CSQB = p.sb([128, NF * TT], BF16, f"CSQB{half}", stack=ph2)
        HOUT = p.sb([128, 8 * TT], BF16, f"HOUT{half}", stack=ph2)
        SCRas = [p.sb([128, 512], F32, f"SCRa{half}{j}", stack=ph2)
                 for j in range(2)]
        rstd_sb = p.sb([128, 512], F32, f"rstdsb{half}", stack=ph2)
        rows2 = make_rows(p, TT, stack=ph2)
        rows2["negmu_bf"] = p.sb([1, TT], BF16, f"negmubf{half}", stack=ph2)
        mu_r1w1 = [(lambda oo, w=w1gsum_bf: w[0:1, oo * 128:(oo + 1) * 128],
                    rows2["negmu_bf"][0:1, 0:TT], [rows2["negmu_bf"]])]

        def prep2a(i, half=half, CTXs=CTXs):
            t0 = i * TA
            CTX = CTXs[i % 2]
            for b in range(B):
                srcap = ctx_h[b, :, t0:t0 + TA].rearrange("(c p) t -> p c t", p=128)
                sch.op("sync", lambda e, srcap=srcap, CTX=CTX, b=b: e.dma_start(
                    out=bass.AP(CTX.t, b * TA, [[NF * TT, 128], [TT, NF], [1, TA]]),
                    in_=srcap),
                    writes=[CTX], reads=[ctx_b], is_dma=True,
                    tag=f"ldc{half}_{i}_{b}")

        def main2a(i, half=half, CTXs=CTXs, CSQB=CSQB, HOUT=HOUT, SCRas=SCRas,
                   rstd_sb=rstd_sb, rows2=rows2, w1=w1, biasH=biasH,
                   mu_r1w1=mu_r1w1, OH=OH):
            t0 = i * TA
            CTX = CTXs[i % 2]
            sch.op("vector", lambda e, CTX=CTX: e.tensor_mul(
                CSQB[:, 0:NF * TT], CTX[:, 0:NF * TT], CTX[:, 0:NF * TT]),
                reads=[CTX], writes=[CSQB], tag=f"cs{half}{i}")
            for c in range(NF):
                sch.op("tensor", lambda e, c=c, CTX=CTX: e.matmul(
                    psA2[0:1, 0:TT], C.ones_col_bf[:, 0:1],
                    CTX[:, c * TT:(c + 1) * TT],
                    start=(c == 0), stop=(c == NF - 1)),
                    reads=[CTX, C.ones_col_bf], writes=[psA2],
                    tag=f"cA{half}{i}_{c}")
            for c in range(NF):
                sch.op("tensor", lambda e, c=c: e.matmul(
                    psB2[0:1, 0:TT], C.ones_col_bf[:, 0:1],
                    CSQB[:, c * TT:(c + 1) * TT],
                    start=(c == 0), stop=(c == NF - 1)),
                    reads=[CSQB, C.ones_col_bf], writes=[psB2],
                    tag=f"cB{half}{i}_{c}")
            ND2 = NF * 128
            sch.op("vector", lambda e: e.tensor_scalar_mul(
                rows2["mu"][0:1, 0:TT], psA2[0:1, 0:TT], 1.0 / ND2),
                reads=[psA2], writes=[rows2["mu"]], tag=f"cmu{half}{i}")
            sch.op("vector", lambda e: e.scalar_tensor_tensor(
                out=rows2["var"][0:1, 0:TT], in0=rows2["mu"][0:1, 0:TT],
                scalar=-1.0, in1=rows2["mu"][0:1, 0:TT],
                op0=ALU.mult, op1=ALU.mult),
                reads=[rows2["mu"]], writes=[rows2["var"]], tag=f"cm2{half}{i}")
            sch.op("vector", lambda e: e.scalar_tensor_tensor(
                out=rows2["var"][0:1, 0:TT], in0=psB2[0:1, 0:TT],
                scalar=1.0 / ND2, in1=rows2["var"][0:1, 0:TT],
                op0=ALU.mult, op1=ALU.add),
                reads=[psB2, rows2["var"]], writes=[rows2["var"]],
                tag=f"cv{half}{i}")
            sch.op("scalar", lambda e: e.activation(
                rows2["rstd"][0:1, 0:TT], rows2["var"][0:1, 0:TT], AF.Ln,
                bias=C.eps5[0:1, 0:1]),
                reads=[rows2["var"], C.eps5], writes=[rows2["rstd"]],
                tag=f"cl{half}{i}")
            sch.op("scalar", lambda e: e.activation(
                rows2["rstd"][0:1, 0:TT], rows2["rstd"][0:1, 0:TT], AF.Exp,
                scale=-0.5),
                reads=[rows2["rstd"]], writes=[rows2["rstd"]], tag=f"ce{half}{i}")
            sch.op("vector", lambda e: e.tensor_scalar_mul(
                rows2["negmu_bf"][0:1, 0:TT], rows2["mu"][0:1, 0:TT], -1.0),
                reads=[rows2["mu"]], writes=[rows2["negmu_bf"]],
                tag=f"cnb{half}{i}")
            sch.op("tensor", lambda e: e.matmul(
                ps_bc2[:, 0:TT], C.ones_row_f32[0:1, :], rows2["rstd"][0:1, 0:TT],
                start=True, stop=True),
                reads=[rows2["rstd"], C.ones_row_f32], writes=[ps_bc2],
                tag=f"cbc{half}{i}")
            sch.op("vector", lambda e: e.tensor_copy(
                rstd_sb[:, 0:TT], ps_bc2[:, 0:TT]),
                reads=[ps_bc2], writes=[rstd_sb], tag=f"cbs{half}{i}")

            def evac_h(oo, psum):
                SCRa = SCRas[oo % 2]
                sch.op("vector", lambda e, oo=oo, psum=psum, SCRa=SCRa: e.tensor_mul(
                    SCRa[:, 0:TT], psum[:, 0:TT], rstd_sb[:, 0:TT]),
                    reads=[psum, rstd_sb], writes=[SCRa], tag=f"hm{half}{i}o{oo}")
                sch.op("scalar", lambda e, oo=oo, SCRa=SCRa: e.activation(
                    HOUT[:, oo * TT:(oo + 1) * TT], SCRa[:, 0:TT], AF.Gelu,
                    bias=biasH[:, oo:oo + 1]),
                    reads=[SCRa, biasH], writes=[HOUT], tag=f"hg{half}{i}o{oo}")
            emit_mm_group(p, w1, CTX, TT, NF, OH, (ps_n0, ps_n1), evac_h,
                          rank1s=mu_r1w1, tag=f"h{half}{i}")
            for b in range(B):
                dst = hbuf_h[b, half * OH:(half + 1) * OH, t0:t0 + TA].rearrange(
                    "(c p) t -> p c t", p=128)
                sch.op("scalar", lambda e, dst=dst, b=b: e.dma_start(
                    out=dst,
                    in_=bass.AP(HOUT.t, b * TA, [[8 * TT, 128], [TT, 8], [1, TA]])),
                    reads=[HOUT], writes=[hb_b], is_dma=True,
                    tag=f"sh{half}_{i}_{b}")

        prep2a(0)
        for i in range(NPTA):
            if i + 1 < NPTA:
                prep2a(i + 1)
            main2a(i)
        sch.flush()
        ph2.close()
        sch.barrier()

    ph3 = ExitStack()
    w2 = load_wT(p, w2T_h, 2 * D, D, "w2", stack=ph3)
    b2c = load_cols(p, b2_h, NCH, "b2c", stack=ph3)
    psA3 = p.ps([128, 512], "psA3", stack=ph3)
    psB3 = p.ps([128, 512], "psB3", stack=ph3)
    ps_bc3 = p.ps([128, 512], "psbc3", stack=ph3)
    ps_n3 = p.ps([128, 512], "psn3", stack=ph3)
    ps_n4 = p.ps([128, 512], "psn4", stack=ph3)
    ps_u3 = p.ps([128, 512], "psu3", stack=ph3)
    HBs = [p.sb([128, 16 * TT], BF16, f"HB{j}", stack=ph3) for j in range(2)]
    X2s = [p.sb([128, NCH * TT], F32, f"X2{j}", stack=ph3) for j in range(2)]
    XB2s = [p.sb([128, NCH * TT], BF16, f"XB2{j}", stack=ph3) for j in range(2)]
    SQB2s = [p.sb([128, NCH * TT], BF16, f"SQB2{j}", stack=ph3) for j in range(2)]
    XLN2 = p.sb([128, NCH * TT], F32, "XLN2", stack=ph3)
    rows3 = make_rows(p, TT, stack=ph3)

    def prep3(i):
        t0 = i * TA
        HB = HBs[i % 2]
        for b in range(B):
            srcap = hbuf_h[b, :, t0:t0 + TA].rearrange("(c p) t -> p c t", p=128)
            sch.op("sync", lambda e, srcap=srcap, HB=HB, b=b: e.dma_start(
                out=bass.AP(HB.t, b * TA, [[16 * TT, 128], [TT, 16], [1, TA]]),
                in_=srcap),
                writes=[HB], reads=[hb_b], is_dma=True, tag=f"lh{i}_{b}")
            srcx = x_h[b, :, t0:t0 + TA].rearrange("(c p) t -> p c t", p=128)
            X2 = X2s[i % 2]
            sch.op("sync", lambda e, srcx=srcx, X2=X2, b=b: e.dma_start(
                out=bass.AP(X2.t, b * TA, [[NCH * TT, 128], [TT, NCH], [1, TA]]),
                in_=srcx),
                writes=[X2], reads=[xin_b], is_dma=True, tag=f"lx{i}_{b}")
        emit_ln_prep(p, X2s[i % 2], TT, NCH, XB2s[i % 2], SQB2s[i % 2],
                     tag=f"l3p{i}")

    def main3(i):
        t0 = i * TA
        HB, X2, XB2, SQB2 = HBs[i % 2], X2s[i % 2], XB2s[i % 2], SQB2s[i % 2]
        emit_ln_main(p, C, TT, NCH, XB2, SQB2, psA3, psB3, ps_bc3, rows3,
                     tag=f"l3{i}")
        sch.op("tensor", lambda e: e.matmul(
            ps_u3[:, 0:TT], C.ones_row_f32[0:1, :],
            rows3["negmurstd"][0:1, 0:TT], start=True, stop=True),
            reads=[rows3["negmurstd"], C.ones_row_f32], writes=[ps_u3],
            tag=f"n3{i}")
        emit_xtilde(p, X2, ps_bc3, XLN2, TT, NCH, tag=f"x3{i}")
        for c in range(NCH):
            sch.op("vector", lambda e, c=c: e.tensor_add(
                XLN2[:, c * TT:(c + 1) * TT], XLN2[:, c * TT:(c + 1) * TT],
                ps_u3[:, 0:TT]),
                reads=[XLN2, ps_u3], writes=[XLN2], tag=f"x3a{i}{c}")

        def evac_o(oo, psum):
            sch.op("vector", lambda e, oo=oo, psum=psum: e.scalar_tensor_tensor(
                out=X2[:, oo * TT:(oo + 1) * TT], in0=psum[:, 0:TT],
                scalar=b2c[:, oo:oo + 1], in1=X2[:, oo * TT:(oo + 1) * TT],
                op0=ALU.add, op1=ALU.add),
                reads=[psum, b2c, X2], writes=[X2], tag=f"o1{i}{oo}")
            sch.op("vector", lambda e, oo=oo: e.tensor_add(
                X2[:, oo * TT:(oo + 1) * TT], X2[:, oo * TT:(oo + 1) * TT],
                XLN2[:, oo * TT:(oo + 1) * TT]),
                reads=[X2, XLN2], writes=[X2], tag=f"o2{i}{oo}")
        emit_mm_group(p, w2, HB, TT, 16, D, (ps_n3, ps_n4), evac_o,
                      tag=f"o{i}")
        for b in range(B):
            dstx = xout_h[b, :, t0:t0 + TA].rearrange("(c p) t -> p c t", p=128)
            sch.op("scalar", lambda e, dstx=dstx, X2=X2, b=b: e.dma_start(
                out=dstx,
                in_=bass.AP(X2.t, b * TA, [[NCH * TT, 128], [TT, NCH], [1, TA]])),
                reads=[X2], writes=[xout_b], is_dma=True, tag=f"so{i}_{b}")

    prep3(0)
    for i in range(NPTA):
        if i + 1 < NPTA:
            prep3(i + 1)
        main3(i)

    sch.flush()
    ph3.close()
    sch.barrier()


NAMES_L0F = ["L0_ffn_W_rT", "L0_ffn_W_kT", "L0_ffn_W_vT",
             "L0_ffn_mu_r_c", "L0_ffn_mu_k_c"]


def emit_l0_ffn(p, C, io, T=512):
    sch = p.sched
    p.prefix = "F_"
    xin_b, xout_b = io["xin"], io["xout"]
    x_h, xout_h = xin_b.t, xout_b.t
    wrT_h = io["L0_ffn_W_rT"]
    wkT_h = io["L0_ffn_W_kT"]
    wvT_h = io["L0_ffn_W_vT"]
    mur_h = io["L0_ffn_mu_r_c"]
    muk_h = io["L0_ffn_mu_k_c"]
    ph = ExitStack()
    _old_sb, _old_ps = p.sb, p.ps
    p.sb = lambda *a, stack=None, **k: _old_sb(*a, stack=(stack or ph), **k)
    p.ps = lambda *a, stack=None, **k: _old_ps(*a, stack=(stack or ph), **k)

    W = NCH * T
    NPT = S // T

    wr = load_wT(p, wrT_h, D, D, "wr")
    wk = load_wT(p, wkT_h, D, D, "wk")
    wv = load_wT(p, wvT_h, D, D, "wv")
    mur = load_cols(p, mur_h, NCH, "mur")
    muk = load_cols(p, muk_h, NCH, "muk")

    psA = p.ps([128, 512], "psA")
    psB = p.ps([128, 512], "psB")
    ps_bc = p.ps([128, 512], "ps_bc")
    ps_m0 = p.ps([128, 512], "ps_m0")
    ps_m1 = p.ps([128, 512], "ps_m1")
    ps_u = p.ps([128, 512], "ps_u")

    Xs = [p.sb([128, W], F32, f"X{j}") for j in range(2)]
    XBs = [p.sb([128, W], BF16, f"XB{j}") for j in range(2)]
    SQBs = [p.sb([128, W], BF16, f"SQB{j}") for j in range(2)]
    XLN = p.sb([128, W], F32, "XLN")
    XPREV = p.sb([128, W], F32, "XPREV")
    DD = p.sb([128, W], F32, "DD")
    XR = p.sb([128, W], BF16, "XR")
    XK = p.sb([128, W], BF16, "XK")
    R = p.sb([128, W], F32, "R")
    KA = XR
    OUT = DD
    rows = make_rows(p, T)
    xcar = p.sb([128, NCH * B], F32, "xcar")
    sch.op("vector", lambda e: e.memset(xcar.ap, 0.0), writes=[xcar], tag="xc0")

    tiles = [(pt, b) for pt in range(NPT) for b in range(B)]

    def prep(i):
        pt, b = tiles[i]
        t0 = pt * T
        X, XB, SQB = Xs[i % 2], XBs[i % 2], SQBs[i % 2]
        load_tile(p, X, x_h[b, :, t0:t0 + T], NCH, T, tag=f"x{pt}_{b}",
                  hbuf=xin_b)
        emit_ln_prep(p, X, T, NCH, XB, SQB, tag=f"lp{pt}_{b}")

    def main(i):
        pt, b = tiles[i]
        t0 = pt * T
        X, XB, SQB = Xs[i % 2], XBs[i % 2], SQBs[i % 2]
        emit_ln_main(p, C, T, NCH, XB, SQB, psA, psB, ps_bc, rows,
                     tag=f"ln{pt}_{b}")
        sch.op("tensor", lambda e: e.matmul(
            ps_u[:, 0:T], C.ones_row_f32[0:1, :], rows["negmurstd"][0:1, 0:T],
            start=True, stop=True),
            reads=[rows["negmurstd"], C.ones_row_f32], writes=[ps_u],
            tag=f"nmb{pt}_{b}")
        emit_xtilde(p, X, ps_bc, XLN, T, NCH, tag=f"xl{pt}_{b}")
        for c in range(NCH):
            sch.op("vector", lambda e, c=c: e.tensor_add(
                XLN[:, c * T:(c + 1) * T], XLN[:, c * T:(c + 1) * T],
                ps_u[:, 0:T]),
                reads=[XLN, ps_u], writes=[XLN], tag=f"xa{pt}_{b}{c}")
        sch.op("vector", lambda e, b=b: e.tensor_copy(
            bass.AP(XPREV.t, 0, [[W, 128], [T, NCH]]),
            xcar[:, b * NCH:(b + 1) * NCH]),
            reads=[xcar], writes=[XPREV], tag=f"shc{pt}_{b}")
        for c in range(NCH):
            sch.op("vector", lambda e, c=c: e.tensor_copy(
                XPREV[:, c * T + 1:(c + 1) * T], XLN[:, c * T:(c + 1) * T - 1]),
                reads=[XLN], writes=[XPREV], tag=f"shf{pt}_{b}{c}")
        sch.op("vector", lambda e, b=b: e.tensor_copy(
            xcar[:, b * NCH:(b + 1) * NCH],
            bass.AP(XLN.t, T - 1, [[W, 128], [T, NCH]])),
            reads=[XLN], writes=[xcar], tag=f"cu{pt}_{b}")
        sch.op("vector", lambda e: e.tensor_sub(DD[:, 0:W], XLN[:, 0:W],
                                                XPREV[:, 0:W]),
               reads=[XLN, XPREV], writes=[DD], tag=f"d{pt}_{b}")
        for c in range(NCH):
            sch.op("vector", lambda e, c=c: e.scalar_tensor_tensor(
                out=XR[:, c * T:(c + 1) * T], in0=DD[:, c * T:(c + 1) * T],
                scalar=mur[:, c:c + 1], in1=XPREV[:, c * T:(c + 1) * T],
                op0=ALU.mult, op1=ALU.add),
                reads=[DD, mur, XPREV], writes=[XR], tag=f"xr{pt}_{b}{c}")
        for c in range(NCH):
            sch.op("vector", lambda e, c=c: e.scalar_tensor_tensor(
                out=XK[:, c * T:(c + 1) * T], in0=DD[:, c * T:(c + 1) * T],
                scalar=muk[:, c:c + 1], in1=XPREV[:, c * T:(c + 1) * T],
                op0=ALU.mult, op1=ALU.add),
                reads=[DD, muk, XPREV], writes=[XK], tag=f"xk{pt}_{b}{c}")

        def evac_r(oo, psum):
            sch.op("scalar", lambda e, oo=oo, psum=psum: e.activation(
                R[:, oo * T:(oo + 1) * T], psum[:, 0:T], AF.Sigmoid),
                reads=[psum], writes=[R], tag=f"rv{pt}_{b}{oo}")
        emit_mm_group(p, wr, XR, T, NCH, D, (ps_m0, ps_m1), evac_r,
                      tag=f"r{pt}_{b}")

        def evac_k(oo, psum):
            sch.op("scalar", lambda e, oo=oo, psum=psum: e.activation(
                XLN[:, oo * T:(oo + 1) * T], psum[:, 0:T], AF.Relu),
                reads=[psum], writes=[XLN], tag=f"kv{pt}_{b}{oo}")
        emit_mm_group(p, wk, XK, T, NCH, D, (ps_m0, ps_m1), evac_k,
                      tag=f"k{pt}_{b}")
        sch.op("scalar", lambda e: e.activation(KA[:, 0:W], XLN[:, 0:W],
                                                AF.Square),
               reads=[XLN], writes=[KA], tag=f"ka{pt}_{b}")

        def evac_v(oo, psum):
            sch.op("vector", lambda e, oo=oo, psum=psum: e.tensor_mul(
                OUT[:, oo * T:(oo + 1) * T], psum[:, 0:T],
                R[:, oo * T:(oo + 1) * T]),
                reads=[psum, R], writes=[OUT], tag=f"vv{pt}_{b}{oo}")
            sch.op("vector", lambda e, oo=oo: e.tensor_add(
                OUT[:, oo * T:(oo + 1) * T], OUT[:, oo * T:(oo + 1) * T],
                X[:, oo * T:(oo + 1) * T]),
                reads=[OUT, X], writes=[OUT], tag=f"vr{pt}_{b}{oo}")
        emit_mm_group(p, wv, KA, T, NCH, D, (ps_m0, ps_m1), evac_v,
                      tag=f"v{pt}_{b}")
        store_tile(p, xout_h[b, :, t0:t0 + T], OUT, NCH, T, tag=f"so{pt}_{b}",
                   hbuf=xout_b)

    prep(0)
    for i in range(len(tiles)):
        if i + 1 < len(tiles):
            prep(i + 1)
        main(i)

    sch.flush()
    p.sb, p.ps = _old_sb, _old_ps
    ph.close()
    sch.barrier()


NAMES_L1A = ["L1_attn_W_bT", "L1_attn_W_cT", "L1_attn_log_a_c",
             "L1_attn_gamma_c", "L1_attn_beta_c", "pos_row"]


def emit_l1_attn(p, C, io, T=512):
    sch = p.sched
    p.prefix = "G_"
    xin_b, xout_b = io["xin"], io["xout"]
    x_h, xout_h = xin_b.t, xout_b.t
    wbT_h = io["L1_attn_W_bT"]
    wcT_h = io["L1_attn_W_cT"]
    loga_h = io["L1_attn_log_a_c"]
    gam_h = io["L1_attn_gamma_c"]
    bet_h = io["L1_attn_beta_c"]
    pos_h = io["pos_row"]
    ph = ExitStack()
    _old_sb, _old_ps = p.sb, p.ps
    p.sb = lambda *a, stack=None, **k: _old_sb(*a, stack=(stack or ph), **k)
    p.ps = lambda *a, stack=None, **k: _old_ps(*a, stack=(stack or ph), **k)

    W = NCH * T
    NPT = S // T

    wb = load_wT(p, wbT_h, D, D, "wb")
    wc = load_wT(p, wcT_h, D, D, "wc")
    loga = load_cols(p, loga_h, NCH, "loga")
    gam = load_cols(p, gam_h, NCH, "gam")
    bet = load_cols(p, bet_h, NCH, "bet")
    pos = p.sb([1, 516], F32, "posr")
    sch.op("sync", lambda e: e.dma_start(out=pos[0:1, 0:512],
                                         in_=pos_h[0:1, 1:513]),
           writes=[pos], is_dma=True, tag="pos_ld")

    psA = p.ps([128, 512], "psA")
    psB = p.ps([128, 512], "psB")
    ps_bc = p.ps([128, 512], "ps_bc")
    ps_m0 = p.ps([128, 512], "ps_m0")
    ps_m1 = p.ps([128, 512], "ps_m1")
    ps_u = p.ps([128, 512], "ps_u")

    wbsum_bf = emit_wsum(p, C, wb, NCH, D, "wbsum", ps_m0)

    acol = p.sb([128, NCH], F32, "acol")
    sch.op("scalar", lambda e: e.activation(acol[:], loga[:], AF.Exp),
           reads=[loga], writes=[acol], tag="a_exp")
    aTc = p.sb([128, NCH], F32, "aTc")
    sch.op("scalar", lambda e: e.activation(aTc[:], loga[:], AF.Exp,
                                            scale=float(T)),
           reads=[loga], writes=[aTc], tag="aT_exp")
    ABC = p.sb([128, W], F32, "ABC")
    APOW = p.sb([128, W], F32, "APOW")
    RAWS = p.sb([128, 512], F32, "RAWS")
    ZW = RAWS
    sch.op("vector", lambda e: e.memset(ZW.ap, 0.0), writes=[ZW], tag="zw")
    for c in range(NCH):
        sch.op("scalar", lambda e, c=c: e.activation(
            ABC[:, c * T:(c + 1) * T], ZW[:, 0:T], AF.Identity,
            bias=acol[:, c:c + 1]),
            reads=[ZW, acol], writes=[ABC], tag=f"abc{c}")
    for c in range(NCH):
        sch.op("tensor", lambda e: e.matmul(
            ps_u[:, 0:T], C.ones_row_f32[0:1, :], pos[0:1, 0:T],
            start=True, stop=True),
            reads=[pos, C.ones_row_f32], writes=[ps_u], tag=f"apm{c}")
        sch.op("scalar", lambda e, c=c: e.activation(
            APOW[:, c * T:(c + 1) * T], ps_u[:, 0:T], AF.Exp,
            scale=loga[:, c:c + 1]),
            reads=[ps_u, loga], writes=[APOW], tag=f"ape{c}")

    Xs = [p.sb([128, W], F32, f"X{j}") for j in range(2)]
    XBs = [p.sb([128, W], BF16, f"XB{j}") for j in range(2)]
    SQBs = [p.sb([128, W], BF16, f"SQB{j}") for j in range(2)]
    STB = p.sb([128, W], BF16, "STB")
    Y = p.sb([128, W], F32, "Y")
    SCRt = p.sb([128, W], F32, "SCRt")
    UBS = SCRt
    rstd_sb = p.sb([128, 512], F32, "grstd")
    YSQ = STB
    rows = make_rows(p, T)
    rows["negmu_bf"] = p.sb([1, T], BF16, "gnegmu")
    rows_y = make_rows(p, T)
    scar = p.sb([128, NCH * B], F32, "scar")
    sch.op("vector", lambda e: e.memset(scar.ap, 0.0), writes=[scar], tag="sc0")

    mu_r1 = [(lambda oo, w=wbsum_bf: w[0:1, oo * 128:(oo + 1) * 128],
              rows["negmu_bf"][0:1, 0:T], [rows["negmu_bf"]])]

    tiles = [(pt, b) for pt in range(NPT) for b in range(B)]

    def prep(i):
        pt, b = tiles[i]
        t0 = pt * T
        X, XB, SQB = Xs[i % 2], XBs[i % 2], SQBs[i % 2]
        load_tile(p, X, x_h[b, :, t0:t0 + T], NCH, T, tag=f"x{pt}_{b}",
                  hbuf=xin_b)
        emit_ln_prep(p, X, T, NCH, XB, SQB, tag=f"lp{pt}_{b}")

    def main(i):
        pt, b = tiles[i]
        t0 = pt * T
        X, XB, SQB = Xs[i % 2], XBs[i % 2], SQBs[i % 2]
        YB = SQB
        emit_ln_main(p, C, T, NCH, XB, SQB, psA, psB, ps_bc, rows,
                     tag=f"ln{pt}_{b}")
        sch.op("vector", lambda e: e.tensor_scalar_mul(
            rows["negmu_bf"][0:1, 0:T], rows["mu"][0:1, 0:T], -1.0),
            reads=[rows["mu"]], writes=[rows["negmu_bf"]], tag=f"gnm{pt}_{b}")
        sch.op("vector", lambda e: e.tensor_copy(rstd_sb[:, 0:T], ps_bc[:, 0:T]),
               reads=[ps_bc], writes=[rstd_sb], tag=f"grs{pt}_{b}")

        def evac_ub(oo, psum):
            c = oo
            cc_ = np.s_[:, c * T:(c + 1) * T]
            col = np.s_[:, b * NCH + c: b * NCH + c + 1]
            sch.op("vector", lambda e, psum=psum: e.tensor_mul(
                UBS[:, 0:T], psum[:, 0:T], rstd_sb[:, 0:T]),
                reads=[psum, rstd_sb], writes=[UBS], tag=f"ubm{pt}_{b}{c}")
            sch.op("vector", lambda e, cc_=cc_: e.tensor_tensor_scan(
                out=RAWS[:, 0:T], data0=ABC[cc_], data1=UBS[:, 0:T],
                initial=0.0, op0=ALU.mult, op1=ALU.add),
                reads=[ABC, UBS], writes=[RAWS], tag=f"em{pt}_{b}{c}")
            sch.op("vector", lambda e, cc_=cc_, col=col: e.scalar_tensor_tensor(
                out=STB[cc_], in0=APOW[cc_], scalar=scar[col],
                in1=RAWS[:, 0:T], op0=ALU.mult, op1=ALU.add),
                reads=[APOW, scar, RAWS], writes=[STB], tag=f"ec{pt}_{b}{c}")
            sch.op("vector", lambda e, c=c, col=col: e.scalar_tensor_tensor(
                out=scar[col], in0=scar[col], scalar=aTc[:, c:c + 1],
                in1=RAWS[:, T - 1:T], op0=ALU.mult, op1=ALU.add),
                reads=[scar, aTc, RAWS], writes=[scar], tag=f"eu{pt}_{b}{c}")
        emit_mm_group(p, wb, XB, T, NCH, D, (ps_m0, ps_m1), evac_ub,
                      rank1s=mu_r1, tag=f"ub{pt}_{b}")

        def evac_y(oo, psum):
            sch.op("vector", lambda e, oo=oo, psum=psum: e.tensor_copy(
                Y[:, oo * T:(oo + 1) * T], psum[:, 0:T]),
                reads=[psum], writes=[Y], tag=f"yv{pt}_{b}{oo}")
        emit_mm_group(p, wc, STB, T, NCH, D, (ps_m0, ps_m1), evac_y,
                      tag=f"y{pt}_{b}")

        emit_ln(p, C, Y, T, NCH, YB, YSQ, psA, psB, ps_bc, rows_y,
                tag=f"lny{pt}_{b}")
        sch.op("tensor", lambda e: e.matmul(
            ps_u[:, 0:T], C.ones_row_f32[0:1, :], rows_y["mu"][0:1, 0:T],
            start=True, stop=True),
            reads=[rows_y["mu"], C.ones_row_f32], writes=[ps_u], tag=f"myb{pt}_{b}")
        for c in range(NCH):
            sch.op("vector", lambda e, c=c: e.tensor_sub(
                SCRt[:, c * T:(c + 1) * T], Y[:, c * T:(c + 1) * T],
                ps_u[:, 0:T]),
                reads=[Y, ps_u], writes=[SCRt], tag=f"ys{pt}_{b}{c}")
        for c in range(NCH):
            sch.op("vector", lambda e, c=c: e.tensor_mul(
                SCRt[:, c * T:(c + 1) * T], SCRt[:, c * T:(c + 1) * T],
                ps_bc[:, 0:T]),
                reads=[SCRt, ps_bc], writes=[SCRt], tag=f"yr{pt}_{b}{c}")
        for c in range(NCH):
            sch.op("scalar", lambda e, c=c: e.activation(
                SCRt[:, c * T:(c + 1) * T], SCRt[:, c * T:(c + 1) * T],
                AF.Identity, scale=gam[:, c:c + 1], bias=bet[:, c:c + 1]),
                reads=[SCRt, gam, bet], writes=[SCRt], tag=f"yg{pt}_{b}{c}")
        sch.op("vector", lambda e: e.tensor_add(SCRt[:, 0:W], SCRt[:, 0:W],
                                                X[:, 0:W]),
               reads=[SCRt, X], writes=[SCRt], tag=f"or{pt}_{b}")
        store_tile(p, xout_h[b, :, t0:t0 + T], SCRt, NCH, T, tag=f"so{pt}_{b}",
                   hbuf=xout_b)

    prep(0)
    for i in range(len(tiles)):
        if i + 1 < len(tiles):
            prep(i + 1)
        main(i)

    sch.flush()
    p.sb, p.ps = _old_sb, _old_ps
    ph.close()
    sch.barrier()


NAMES_L1F = ["L1_ffn_w1T", "L1_ffn_w2T", "L1_ffn_h1T", "L1_ffn_h2T",
             "L1_ffn_w3T", "L1_ffn_w1_s_c", "L1_ffn_w2_s_c", "L1_ffn_h1_s_c",
             "L1_ffn_h2_s_c", "L1_ffn_w3_s_c"]


def emit_l1_ffn(p, C, io, T=512):
    sch = p.sched
    p.prefix = "H_"
    xin_b, xout_b = io["xin"], io["xout"]
    x_h, xout_h = xin_b.t, xout_b.t
    w1T_h = io["L1_ffn_w1T"]
    w2T_h = io["L1_ffn_w2T"]
    h1T_h = io["L1_ffn_h1T"]
    h2T_h = io["L1_ffn_h2T"]
    w3T_h = io["L1_ffn_w3T"]
    s1_h = io["L1_ffn_w1_s_c"]
    s2_h = io["L1_ffn_w2_s_c"]
    h1s_h = io["L1_ffn_h1_s_c"]
    h2s_h = io["L1_ffn_h2_s_c"]
    w3s_h = io["L1_ffn_w3_s_c"]
    hb_h = p.dram_tmp("hbuf", [B, 4 * D, S], BF16)
    g1_h = p.dram_tmp("g1buf", [B, 4 * D, S], BF16)
    g2_h = p.dram_tmp("g2buf", [B, 4 * D, S], BF16)
    hb_b = Buf(hb_h, "hbuf"); g1_b = Buf(g1_h, "g1buf"); g2_b = Buf(g2_h, "g2buf")
    buf_of = {id(hb_h): hb_b, id(g1_h): g1_b, id(g2_h): g2_b}

    W = NCH * T
    W4 = 32 * T
    NPT = S // T
    TA = 256
    WA = NCH * TA
    WA4 = 32 * TA
    NPTA = S // TA

    TA = 256
    TT = 2 * TA
    WA = NCH * TT
    NPTA = S // TA
    for ah in range(2):
        ph = ExitStack()
        OHA = 2 * D
        w1 = p.sb([128, NCH * OHA], BF16, f"w1h{ah}", stack=ph)
        w2 = p.sb([128, NCH * OHA], BF16, f"w2h{ah}", stack=ph)
        for kc in range(NCH):
            sch.op("gpsimd", lambda e, kc=kc: e.dma_start(
                out=w1[:, kc * OHA:(kc + 1) * OHA],
                in_=w1T_h[kc * 128:(kc + 1) * 128, ah * OHA:(ah + 1) * OHA]),
                writes=[w1], is_dma=True, tag=f"aw1l{ah}_{kc}")
            sch.op("gpsimd", lambda e, kc=kc: e.dma_start(
                out=w2[:, kc * OHA:(kc + 1) * OHA],
                in_=w2T_h[kc * 128:(kc + 1) * 128, ah * OHA:(ah + 1) * OHA]),
                writes=[w2], is_dma=True, tag=f"aw2l{ah}_{kc}")
        s1 = load_cols(p, s1_h, 32, f"s1_{ah}", stack=ph)
        s2 = load_cols(p, s2_h, 32, f"s2_{ah}", stack=ph)
        if ah == 0:
            psA = p.ps([128, 512], "psA")
            psB = p.ps([128, 512], "psB")
            ps_bc = p.ps([128, 512], "ps_bc")
            ps_m0 = p.ps([128, 512], "ps_m0")
            ps_m1 = p.ps([128, 512], "ps_m1")
            ps_m2 = p.ps([128, 512], "ps_m2")
            ps_m3 = p.ps([128, 512], "ps_m3")
        w1sum_bf = emit_wsum(p, C, w1, NCH, OHA, f"w1sum{ah}", ps_m0, stack=ph)
        w2sum_bf = emit_wsum(p, C, w2, NCH, OHA, f"w2sum{ah}", ps_m0, stack=ph)

        Xs = [p.sb([128, WA], F32, f"aX{ah}{j}", stack=ph) for j in range(2)]
        XBs = [p.sb([128, WA], BF16, f"aXB{ah}{j}", stack=ph) for j in range(2)]
        SQBs = [p.sb([128, WA], BF16, f"aSQB{ah}{j}", stack=ph) for j in range(2)]
        SILs = [p.sb([128, 512], F32, f"aSIL{ah}{j}", stack=ph) for j in range(2)]
        T2s = [p.sb([128, 512], F32, f"aT2{ah}{j}", stack=ph) for j in range(2)]
        HOUT = p.sb([128, 16 * TT], BF16, f"aH{ah}", stack=ph)
        rows = make_rows(p, TT, stack=ph)
        rows["negmu_bf"] = p.sb([1, TT], BF16, f"anegmu{ah}", stack=ph)
        rstd_sb = p.sb([128, 512], F32, f"arstd{ah}", stack=ph)

        def prepA(i, ah=ah, Xs=Xs, XBs=XBs, SQBs=SQBs):
            t0 = i * TA
            X = Xs[i % 2]
            for b in range(B):
                srcap = x_h[b, :, t0:t0 + TA].rearrange("(c p) t -> p c t", p=128)
                sch.op("sync", lambda e, srcap=srcap, X=X, b=b: e.dma_start(
                    out=bass.AP(X.t, b * TA, [[WA, 128], [TT, NCH], [1, TA]]),
                    in_=srcap),
                    writes=[X], reads=[xin_b], is_dma=True, tag=f"ax{ah}_{i}_{b}")
            emit_ln_prep(p, X, TT, NCH, XBs[i % 2], SQBs[i % 2], tag=f"alp{ah}_{i}")

        def mainA(i, ah=ah, Xs=Xs, XBs=XBs, SQBs=SQBs, SILs=SILs, T2s=T2s,
                  HOUT=HOUT, rows=rows, w1=w1, w2=w2, s1=s1, s2=s2,
                  w1sum_bf=w1sum_bf, w2sum_bf=w2sum_bf, OHA=OHA):
            t0 = i * TA
            X, XB, SQB = Xs[i % 2], XBs[i % 2], SQBs[i % 2]
            emit_ln_main(p, C, TT, NCH, XB, SQB, psA, psB, ps_bc, rows,
                         tag=f"aln{ah}_{i}")
            sch.op("vector", lambda e: e.tensor_scalar_mul(
                rows["negmu_bf"][0:1, 0:TT], rows["mu"][0:1, 0:TT], -1.0),
                reads=[rows["mu"]], writes=[rows["negmu_bf"]], tag=f"anm{ah}_{i}")
            sch.op("vector", lambda e: e.tensor_copy(
                rstd_sb[:, 0:TT], ps_bc[:, 0:TT]),
                reads=[ps_bc], writes=[rstd_sb], tag=f"ars{ah}_{i}")
            for oi in range(16):
                oo = ah * 16 + oi
                pa = (ps_m0, ps_m2)[oi % 2]
                pb = (ps_m1, ps_m3)[oi % 2]
                for kc in range(NCH):
                    sch.op("tensor", lambda e, kc=kc, oi=oi, pa=pa, XB=XB: e.matmul(
                        pa[:, 0:TT],
                        w1[:, kc * OHA + oi * 128: kc * OHA + oi * 128 + 128],
                        XB[:, kc * TT:(kc + 1) * TT],
                        start=(kc == 0), stop=False),
                        reads=[w1, XB], writes=[pa], tag=f"am1{ah}_{i}o{oi}k{kc}")
                sch.op("tensor", lambda e, oi=oi, pa=pa: e.matmul(
                    pa[:, 0:TT], w1sum_bf[0:1, oi * 128:(oi + 1) * 128],
                    rows["negmu_bf"][0:1, 0:TT], start=False, stop=True),
                    reads=[w1sum_bf, rows["negmu_bf"]], writes=[pa],
                    tag=f"am1r{ah}_{i}o{oi}")
                for kc in range(NCH):
                    sch.op("tensor", lambda e, kc=kc, oi=oi, pb=pb, XB=XB: e.matmul(
                        pb[:, 0:TT],
                        w2[:, kc * OHA + oi * 128: kc * OHA + oi * 128 + 128],
                        XB[:, kc * TT:(kc + 1) * TT],
                        start=(kc == 0), stop=False),
                        reads=[w2, XB], writes=[pb], tag=f"am2{ah}_{i}o{oi}k{kc}")
                sch.op("tensor", lambda e, oi=oi, pb=pb: e.matmul(
                    pb[:, 0:TT], w2sum_bf[0:1, oi * 128:(oi + 1) * 128],
                    rows["negmu_bf"][0:1, 0:TT], start=False, stop=True),
                    reads=[w2sum_bf, rows["negmu_bf"]], writes=[pb],
                    tag=f"am2r{ah}_{i}o{oi}")
                SIL, T2 = SILs[oi % 2], T2s[oi % 2]
                sch.op("vector", lambda e, pa=pa, SIL=SIL: e.tensor_mul(
                    SIL[:, 0:TT], pa[:, 0:TT], rstd_sb[:, 0:TT]),
                    reads=[pa, rstd_sb], writes=[SIL], tag=f"ay1{ah}_{i}o{oi}")
                sch.op("scalar", lambda e, oo=oo, SIL=SIL: e.activation(
                    SIL[:, 0:TT], SIL[:, 0:TT], AF.Silu, scale=s1[:, oo:oo + 1]),
                    reads=[SIL, s1], writes=[SIL], tag=f"asl{ah}_{i}o{oo}")
                sch.op("vector", lambda e, pb=pb, T2=T2: e.tensor_mul(
                    T2[:, 0:TT], pb[:, 0:TT], rstd_sb[:, 0:TT]),
                    reads=[pb, rstd_sb], writes=[T2], tag=f"ay2{ah}_{i}o{oi}")
                sch.op("vector", lambda e, oi=oi, oo=oo, SIL=SIL, T2=T2: e.scalar_tensor_tensor(
                    out=HOUT[:, oi * TT:(oi + 1) * TT], in0=T2[:, 0:TT],
                    scalar=s2[:, oo:oo + 1], in1=SIL[:, 0:TT],
                    op0=ALU.mult, op1=ALU.mult),
                    reads=[SIL, T2, s2], writes=[HOUT], tag=f"ah{ah}_{i}o{oi}")
            for b in range(B):
                dst = hb_h[b, ah * OHA:(ah + 1) * OHA, t0:t0 + TA].rearrange(
                    "(c p) t -> p c t", p=128)
                sch.op("scalar", lambda e, dst=dst, b=b: e.dma_start(
                    out=dst,
                    in_=bass.AP(HOUT.t, b * TA, [[16 * TT, 128], [TT, 16], [1, TA]])),
                    reads=[HOUT], writes=[hb_b], is_dma=True,
                    tag=f"ash{ah}_{i}_{b}")

        prepA(0)
        for i in range(NPTA):
            if i + 1 < NPTA:
                prepA(i + 1)
            mainA(i)
        sch.flush()
        ph.close()
        sch.barrier()

    def gelu_phase(win_h, sc_h, src_h, dst_h, name):
        OH = D
        src_b = buf_of[id(src_h)]
        dst_b = buf_of[id(dst_h)]
        for qr in range(4):
            ph = ExitStack()
            wt = p.sb([128, 32 * OH], BF16, f"{name}_w{qr}", stack=ph)
            for kc in range(32):
                sch.op("gpsimd", lambda e, kc=kc: e.dma_start(
                    out=wt[:, kc * OH:(kc + 1) * OH],
                    in_=win_h[kc * 128:(kc + 1) * 128,
                              qr * OH:(qr + 1) * OH]),
                    writes=[wt], is_dma=True, tag=f"{name}w{qr}_{kc}")
            scc = load_cols(p, sc_h, 32, f"{name}_s{qr}", stack=ph)
            INs = [p.sb([128, W4], BF16, f"{name}_in{qr}_{j}", stack=ph)
                   for j in range(2)]
            OUTB = p.sb([128, 8 * T], BF16, f"{name}_out{qr}", stack=ph)
            tiles = [(pt, b) for pt in range(NPT) for b in range(B)]

            def prep(i):
                pt, b = tiles[i]
                IN = INs[i % 2]
                t0 = pt * T
                srcap = src_h[b, :, t0:t0 + T].rearrange("(c p) t -> p c t", p=128)
                sch.op("sync", lambda e, srcap=srcap, IN=IN: e.dma_start(
                    out=bass.AP(IN.t, 0, [[W4, 128], [T, 32], [1, T]]), in_=srcap),
                    writes=[IN], reads=[src_b], is_dma=True,
                    tag=f"{name}l{qr}_{i}")

            def main(i):
                pt, b = tiles[i]
                IN = INs[i % 2]
                t0 = pt * T
                for oi in range(8):
                    oo = qr * 8 + oi
                    psum = (ps_m0, ps_m1, ps_m2, ps_m3)[oi % 4]
                    for kc in range(32):
                        sch.op("tensor", lambda e, kc=kc, oi=oi, psum=psum, IN=IN: e.matmul(
                            psum[:, 0:T],
                            wt[:, kc * OH + oi * 128: kc * OH + oi * 128 + 128],
                            IN[:, kc * T:(kc + 1) * T],
                            start=(kc == 0), stop=(kc == 31)),
                            reads=[wt, IN], writes=[psum],
                            tag=f"{name}m{qr}_{i}o{oi}k{kc}")
                    sch.op("scalar", lambda e, oi=oi, oo=oo, psum=psum: e.activation(
                        OUTB[:, oi * T:(oi + 1) * T], psum[:, 0:T], AF.Gelu,
                        scale=scc[:, oo:oo + 1]),
                        reads=[psum, scc], writes=[OUTB],
                        tag=f"{name}g{qr}_{i}o{oi}")
                dst = dst_h[b, qr * OH:(qr + 1) * OH, t0:t0 + T].rearrange(
                    "(c p) t -> p c t", p=128)
                sch.op("scalar", lambda e, dst=dst: e.dma_start(
                    out=dst, in_=bass.AP(OUTB.t, 0, [[8 * T, 128], [T, 8], [1, T]])),
                    reads=[OUTB], writes=[dst_b], is_dma=True,
                    tag=f"{name}s{qr}_{i}")

            prep(0)
            for i in range(len(tiles)):
                if i + 1 < len(tiles):
                    prep(i + 1)
                main(i)
            sch.flush()
            ph.close()
            sch.barrier()

    gelu_phase(h1T_h, h1s_h, hb_h, g1_h, "pB")
    gelu_phase(h2T_h, h2s_h, g1_h, g2_h, "pC")

    ph = ExitStack()
    w3 = load_wT(p, w3T_h, 4 * D, D, "w3", stack=ph)
    s3 = load_cols(p, w3s_h, NCH, "s3", stack=ph)
    INs = [p.sb([128, W4], BF16, f"d_in{j}", stack=ph) for j in range(2)]
    XDs = [p.sb([128, W], F32, f"XD{j}", stack=ph) for j in range(2)]
    OUTD = p.sb([128, W], F32, "OUTD", stack=ph)
    SCRd = p.sb([128, 512], F32, "SCRd", stack=ph)
    tilesD = [(pt, b) for pt in range(NPT) for b in range(B)]

    def prepD(i):
        pt, b = tilesD[i]
        t0 = pt * T
        IN = INs[i % 2]
        srcap = g2_h[b, :, t0:t0 + T].rearrange("(c p) t -> p c t", p=128)
        sch.op("sync", lambda e, srcap=srcap, IN=IN: e.dma_start(
            out=bass.AP(IN.t, 0, [[W4, 128], [T, 32], [1, T]]), in_=srcap),
            writes=[IN], reads=[g2_b], is_dma=True, tag=f"dl{pt}_{b}")
        load_tile(p, XDs[i % 2], x_h[b, :, t0:t0 + T], NCH, T,
                  tag=f"dx{pt}_{b}", hbuf=xin_b)

    def mainD(i):
        pt, b = tilesD[i]
        t0 = pt * T
        IN, XD = INs[i % 2], XDs[i % 2]
        for oo in range(NCH):
            psum = (ps_m0, ps_m1)[oo % 2]
            for kc in range(32):
                sch.op("tensor", lambda e, kc=kc, oo=oo, psum=psum, IN=IN: e.matmul(
                    psum[:, 0:T],
                    w3[:, kc * D + oo * 128: kc * D + oo * 128 + 128],
                    IN[:, kc * T:(kc + 1) * T],
                    start=(kc == 0), stop=(kc == 31)),
                    reads=[w3, IN], writes=[psum], tag=f"dm{pt}_{b}{oo}k{kc}")
            sch.op("scalar", lambda e, oo=oo, psum=psum: e.activation(
                SCRd[:, 0:T], psum[:, 0:T], AF.Identity, scale=s3[:, oo:oo + 1]),
                reads=[psum, s3], writes=[SCRd], tag=f"dsc{pt}_{b}{oo}")
            sch.op("vector", lambda e, oo=oo, XD=XD: e.tensor_add(
                OUTD[:, oo * T:(oo + 1) * T], SCRd[:, 0:T],
                XD[:, oo * T:(oo + 1) * T]),
                reads=[SCRd, XD], writes=[OUTD], tag=f"dr{pt}_{b}{oo}")
        store_tile(p, xout_h[b, :, t0:t0 + T], OUTD, NCH, T, tag=f"ds{pt}_{b}",
                   hbuf=xout_b)

    prepD(0)
    for i in range(len(tilesD)):
        if i + 1 < len(tilesD):
            prepD(i + 1)
        mainD(i)
    sch.flush()
    ph.close()
    sch.barrier()



def cols(v, n):
    return np.ascontiguousarray(v.reshape(n, 128).T)


def prep_weights(inputs):
    f = np.float32
    w = {}
    w["L0_attn_W_magT"] = np.ascontiguousarray(inputs["L0_attn_W_mag"].T)
    w["L0_attn_b_mag_c"] = cols(inputs["L0_attn_b_mag"], NCH)
    w["L0_attn_W_qT"] = np.ascontiguousarray(inputs["L0_attn_W_q"].T)
    w["L0_attn_b_q_c"] = cols(inputs["L0_attn_b_q"], NCH)
    w["L0_attn_random_phases_r"] = inputs["L0_attn_random_phases"].reshape(1, D)
    w["pos_row"] = np.arange(S, dtype=f).reshape(1, S)
    w["L0_attn_W1T"] = np.ascontiguousarray(inputs["L0_attn_W1"].T)
    w["L0_attn_b1_c"] = cols(inputs["L0_attn_b1"], 16)
    w["L0_attn_W2T"] = np.ascontiguousarray(inputs["L0_attn_W2"].T)
    w["L0_attn_b2_c"] = cols(inputs["L0_attn_b2"], NCH)
    w["L0_attn_ln_g_c"] = cols(inputs["L0_attn_ln_g"], 32)
    w["L0_attn_ln_b_c"] = cols(inputs["L0_attn_ln_b"], 32)
    w["L0_ffn_W_rT"] = np.ascontiguousarray(inputs["L0_ffn_W_r"].T)
    w["L0_ffn_W_kT"] = np.ascontiguousarray(inputs["L0_ffn_W_k"].T)
    w["L0_ffn_W_vT"] = np.ascontiguousarray(inputs["L0_ffn_W_v"].T)
    w["L0_ffn_mu_r_c"] = cols(inputs["L0_ffn_mu_r"], NCH)
    w["L0_ffn_mu_k_c"] = cols(inputs["L0_ffn_mu_k"], NCH)
    w["L1_attn_W_bT"] = np.ascontiguousarray(inputs["L1_attn_W_b"].T)
    w["L1_attn_W_cT"] = np.ascontiguousarray(inputs["L1_attn_W_c"].T)
    w["L1_attn_log_a_c"] = cols(inputs["L1_attn_log_a"], NCH)
    w["L1_attn_gamma_c"] = cols(inputs["L1_attn_gamma"], NCH)
    w["L1_attn_beta_c"] = cols(inputs["L1_attn_beta"], NCH)
    w["L1_ffn_w1T"] = np.ascontiguousarray(inputs["L1_ffn_w1_w"].T)
    w["L1_ffn_w2T"] = np.ascontiguousarray(inputs["L1_ffn_w2_w"].T)
    w["L1_ffn_h1T"] = np.ascontiguousarray(inputs["L1_ffn_h1_w"].T)
    w["L1_ffn_h2T"] = np.ascontiguousarray(inputs["L1_ffn_h2_w"].T)
    w["L1_ffn_w3T"] = np.ascontiguousarray(inputs["L1_ffn_w3_w"].T)
    w["L1_ffn_w1_s_c"] = cols(inputs["L1_ffn_w1_s"], 32)
    w["L1_ffn_w2_s_c"] = cols(inputs["L1_ffn_w2_s"], 32)
    w["L1_ffn_h1_s_c"] = cols(inputs["L1_ffn_h1_s"], 32)
    w["L1_ffn_h2_s_c"] = cols(inputs["L1_ffn_h2_s"], 32)
    w["L1_ffn_w3_s_c"] = cols(inputs["L1_ffn_w3_s"], NCH)
    return w


WEIGHT_SHAPES = {
    "L0_attn_W_magT": ([D, D], F32), "L0_attn_b_mag_c": ([128, NCH], F32),
    "L0_attn_W_qT": ([D, D], F32), "L0_attn_b_q_c": ([128, NCH], F32),
    "L0_attn_random_phases_r": ([1, D], F32), "pos_row": ([1, S], F32),
    "L0_attn_W1T": ([4 * D, 2 * D], F32), "L0_attn_b1_c": ([128, 16], F32),
    "L0_attn_W2T": ([2 * D, D], F32), "L0_attn_b2_c": ([128, NCH], F32),
    "L0_attn_ln_g_c": ([128, 32], F32), "L0_attn_ln_b_c": ([128, 32], F32),
    "L0_ffn_W_rT": ([D, D], F32), "L0_ffn_W_kT": ([D, D], F32),
    "L0_ffn_W_vT": ([D, D], F32), "L0_ffn_mu_r_c": ([128, NCH], F32),
    "L0_ffn_mu_k_c": ([128, NCH], F32),
    "L1_attn_W_bT": ([D, D], F32), "L1_attn_W_cT": ([D, D], F32),
    "L1_attn_log_a_c": ([128, NCH], F32), "L1_attn_gamma_c": ([128, NCH], F32),
    "L1_attn_beta_c": ([128, NCH], F32),
    "L1_ffn_w1T": ([D, 4 * D], F32), "L1_ffn_w2T": ([D, 4 * D], F32),
    "L1_ffn_h1T": ([4 * D, 4 * D], F32), "L1_ffn_h2T": ([4 * D, 4 * D], F32),
    "L1_ffn_w3T": ([4 * D, D], F32),
    "L1_ffn_w1_s_c": ([128, 32], F32), "L1_ffn_w2_s_c": ([128, 32], F32),
    "L1_ffn_h1_s_c": ([128, 32], F32), "L1_ffn_h2_s_c": ([128, 32], F32),
    "L1_ffn_w3_s_c": ([128, NCH], F32),
}

EMITTERS = {
    "l0a": (emit_l0_attn, NAMES_L0A),
    "l0f": (emit_l0_ffn, NAMES_L0F),
    "l1a": (emit_l1_attn, NAMES_L1A),
    "l1f": (emit_l1_ffn, NAMES_L1F),
}
BLOCK_ORDER = ["l0a", "l0f", "l1a", "l1f"]


def build_program(blocks, dbg=False):
    p = Prog()
    C = Consts(p, 512)
    names = []
    for bl in blocks:
        for n in EMITTERS[bl][1]:
            if n not in names:
                names.append(n)
    io = {}
    for n in names:
        sh, dt = WEIGHT_SHAPES[n]
        io[n] = p.dram_in(n, sh, dt)
    xin = Buf(p.dram_in("xin", [B, D, S]), "xin")
    xout = Buf(p.dram_out("xout", [B, D, S]), "xout")
    cur = xin
    for i, bl in enumerate(blocks):
        nxt = xout if i == len(blocks) - 1 else Buf(
            p.dram_tmp(f"xmid{i}", [B, D, S]), f"xmid{i}")
        bio = dict(io)
        bio["xin"] = cur
        bio["xout"] = nxt
        kw = {"dbg": dbg} if bl == "l0a" and dbg else {}
        EMITTERS[bl][0](p, C, bio, **kw)
        cur = nxt
    nc = p.finish()
    return nc, ["xin"] + names


def build_l0_attn(T=256, dbg=False):
    return build_program(["l0a"], dbg=dbg)


def build_l0_ffn(T=512):
    return build_program(["l0f"])


def build_l1_attn(T=512):
    return build_program(["l1a"])


def build_l1_ffn(T=512):
    return build_program(["l1f"])


def run_blocks(blocks, x_fm_shards, w):
    nc, names = build_program(blocks)
    in_maps = []
    for i in range(NCORES):
        m = {"xin": x_fm_shards[i]}
        for n in names:
            if n != "xin":
                m[n] = w[n]
        in_maps.append(m)
    res = run_bass_kernel_spmd(nc, in_maps, core_ids=list(range(NCORES)))
    return [res.results[i]["xout"] for i in range(NCORES)]


FUSED = True


def kernel(**inputs):
    w = prep_weights(inputs)
    x = np.asarray(inputs["L0_attn_x"], dtype=np.float32)
    shards = [np.ascontiguousarray(x[2 * i:2 * i + 2].transpose(0, 2, 1))
              for i in range(NCORES)]
    if FUSED:
        shards = run_blocks(BLOCK_ORDER, shards, w)
    else:
        for bl in BLOCK_ORDER:
            shards = [np.asarray(s, dtype=np.float32) for s in shards]
            shards = run_blocks([bl], shards, w)
    out = np.empty((16, S, D), dtype=np.float32)
    for i in range(NCORES):
        out[2 * i:2 * i + 2] = np.asarray(shards[i]).transpose(0, 2, 1)
    return out


# version 09 (final)
# speedup vs baseline: 1.1530x; 1.1530x over previous
import numpy as np
import ml_dtypes
import concourse.bass as bass
import concourse.mybir as mybir
from concourse.bass_utils import run_bass_kernel_spmd
from contextlib import ExitStack

F32 = mybir.dt.float32
BF16 = mybir.dt.bfloat16
AF = mybir.ActivationFunctionType
ALU = mybir.AluOpType
AX = mybir.AxisListType
MAGIC = float(1.5 * 2**23)
TWOPI = float(2.0 * np.pi)
INV2PI = float(1.0 / (2.0 * np.pi))
LN5 = float(np.log(5.0))
NCORES = 8
B = 2
S = 4096
D = 1024
NCH = D // 128


ENGINE_ORDER = ("tensor", "vector", "scalar", "gpsimd", "sync")


class _Op:
    __slots__ = ("engine", "fn", "deps", "is_dma", "idx", "sem_idx", "needs_inc", "tag")

    def __init__(self, engine, fn, is_dma, tag):
        self.engine = engine
        self.fn = fn
        self.deps = []
        self.is_dma = is_dma
        self.idx = None
        self.sem_idx = None
        self.needs_inc = False
        self.tag = tag


class Buf:
    __slots__ = ("t", "writers", "readers", "name")

    def __init__(self, t, name=""):
        self.t = t
        self.writers = []
        self.readers = []
        self.name = name

    def __getitem__(self, key):
        return self.t[key]

    @property
    def ap(self):
        return self.t[:]


class Sched:
    def __init__(self, nc, st):
        self.nc = nc
        self.st = st
        self.ops = []
        self.sems = {}
        self.counts = {}
        self.high = {}
        self.barrier_deps = []
        self.last_op_per_stream = {}

    def op(self, engine, fn, reads=(), writes=(), is_dma=False, tag=""):
        o = _Op(engine, fn, is_dma, tag)
        o.idx = len(self.ops)
        o.deps.extend(self.barrier_deps)
        for b in reads:
            o.deps.extend(b.writers)
            b.readers.append(o)
        for b in writes:
            o.deps.extend(b.readers)
            o.deps.extend(b.writers)
            b.writers = [o]
            b.readers = []
        self.ops.append(o)
        self.last_op_per_stream[(engine, is_dma)] = o
        return o

    def barrier(self):
        self.barrier_deps = list(self.last_op_per_stream.values())

    def _sem_key(self, o):
        return ("dma_" + o.engine) if o.is_dma else ("c_" + o.engine)

    def flush(self):
        nc = self.nc
        ops = self.ops
        if not ops:
            return
        for o in ops:
            for dep in o.deps:
                if dep.is_dma or o.is_dma or dep.engine != o.engine:
                    dep.needs_inc = True
        for o in ops:
            if o.needs_inc or o.is_dma:
                key = self._sem_key(o)
                if key not in self.sems:
                    self.sems[key] = self.st.enter_context(self.nc.semaphore(key))
                    self.counts[key] = 0
                self.counts[key] += 16 if o.is_dma else 1
                o.sem_idx = (key, self.counts[key])
        engines_used = sorted({o.engine for o in ops},
                              key=lambda e: ENGINE_ORDER.index(e))
        with nc.Block() as block:
            for eng_name in engines_used:
                eng_ops = [o for o in ops if o.engine == eng_name]
                high = self.high.setdefault(eng_name, {})

                def body(eng, eng_ops=eng_ops, eng_name=eng_name, high=high):
                    for o in eng_ops:
                        for dep in o.deps:
                            if dep.sem_idx is None:
                                continue
                            if ((not dep.is_dma) and (not o.is_dma)
                                    and dep.engine == eng_name):
                                continue
                            key, val = dep.sem_idx
                            if high.get(key, 0) >= val:
                                continue
                            eng.wait_ge(self.sems[key], val)
                            high[key] = val
                        inst = o.fn(eng)
                        if o.sem_idx is not None:
                            key, _val = o.sem_idx
                            if inst is None:
                                raise RuntimeError(f"op {o.tag} needs inc, emitted nothing")
                            if o.is_dma or o.engine in ("tensor", "sync"):
                                inst.then_inc(self.sems[key], 16 if o.is_dma else 1)
                            else:
                                eng.drain().then_inc(self.sems[key], 1)

                getattr(block, eng_name)(body)
        self.ops = []

    def finish(self):
        self.flush()
        nc = self.nc
        dma_keys = [k for k in self.counts if k.startswith("dma_")]
        with nc.Block() as block:
            def tail(eng):
                for k in dma_keys:
                    eng.wait_ge(self.sems[k], self.counts[k])
            block.sync(tail)


class Prog:
    def __init__(self):
        self.nc = bass.Bass("TRN2", target_bir_lowering=False, debug=False)
        self.st = ExitStack()
        self.sched = Sched(self.nc, self.st)
        self.n = 0
        self.prefix = ""

    def sb(self, shape, dtype, name=None, stack=None):
        self.n += 1
        nm = self.prefix + (name or f"sb{self.n}")
        t = (stack or self.st).enter_context(
            self.nc.sbuf_tensor(nm, shape, dtype))
        return Buf(t, nm)

    def ps(self, shape, name=None, stack=None):
        self.n += 1
        nm = self.prefix + (name or f"ps{self.n}")
        t = (stack or self.st).enter_context(
            self.nc.psum_tensor(nm, shape, F32))
        return Buf(t, nm)

    def dram_in(self, name, shape, dtype=F32):
        return self.nc.dram_tensor(name, shape, dtype, kind="ExternalInput").ap()

    def dram_out(self, name, shape, dtype=F32):
        return self.nc.dram_tensor(name, shape, dtype, kind="ExternalOutput").ap()

    def dram_tmp(self, name, shape, dtype=F32):
        return self.nc.dram_tensor(name, shape, dtype, kind="Internal").ap()

    def finish(self):
        self.sched.finish()
        self.st.close()
        return self.nc



def t3ap(buf, nch, T, p=128):
    W = nch * T
    return bass.AP(buf.t, 0, [[W, p], [T, nch], [1, T]])


def load_tile(p, X, dram_slice, nch, T, queue="sync", tag="ldx", hbuf=None):
    src = dram_slice.rearrange("(c p) t -> p c t", p=128)
    return p.sched.op(queue,
                      lambda e, src=src: e.dma_start(out=t3ap(X, nch, T), in_=src),
                      writes=[X], reads=([hbuf] if hbuf is not None else []),
                      is_dma=True, tag=tag)


def store_tile(p, dram_slice, X, nch, T, queue="scalar", tag="stx", hbuf=None):
    dst = dram_slice.rearrange("(c p) t -> p c t", p=128)
    return p.sched.op(queue,
                      lambda e, dst=dst: e.dma_start(out=dst, in_=t3ap(X, nch, T)),
                      reads=[X], writes=([hbuf] if hbuf is not None else []),
                      is_dma=True, tag=tag)


def load_wT(p, wT_ap, K, O, name, stack=None):
    nk = K // 128
    wb = p.sb([128, nk * O], BF16, name, stack=stack)
    for kc in range(nk):
        p.sched.op("gpsimd",
                   lambda e, kc=kc: e.dma_start(
                       out=wb[:, kc * O:(kc + 1) * O],
                       in_=wT_ap[kc * 128:(kc + 1) * 128, :]),
                   writes=[wb], is_dma=True, tag=f"{name}_ld{kc}")
    return wb


def load_cols(p, ap, n, name, dtype=F32, stack=None):
    cb = p.sb([128, n], dtype, name, stack=stack)
    p.sched.op("sync", lambda e: e.dma_start(out=cb[:], in_=ap[:]),
               writes=[cb], is_dma=True, tag=f"{name}_ld")
    return cb


class Consts:

    def __init__(self, p, T):
        self.T = T
        sch = p.sched
        self.ones_col_bf = p.sb([128, 1], BF16, "ones_col_bf")
        self.ones_row_f32 = p.sb([1, 128], F32, "ones_row_f32")
        self.ones_row_bf = p.sb([1, max(T, 512)], BF16, "ones_row_bf")
        self.eps5 = p.sb([128, 1], F32, "eps5")
        self.eps8 = p.sb([128, 1], F32, "eps8")
        self.ln5 = p.sb([128, 1], F32, "ln5")
        for b_, v in ((self.ones_col_bf, 1.0), (self.ones_row_f32, 1.0),
                      (self.ones_row_bf, 1.0), (self.eps5, 1e-5),
                      (self.eps8, 1e-8), (self.ln5, LN5)):
            sch.op("vector", lambda e, b_=b_, v=v: e.memset(b_.ap, v), writes=[b_],
                   tag="const")


def make_rows(p, T, stack=None):
    rows = {k: p.sb([1, T], F32, None, stack=stack)
            for k in ("mu", "var", "rstd", "negmurstd")}
    rows["negmurstd_bf"] = p.sb([1, T], BF16, None, stack=stack)
    return rows


def emit_ln_prep(p, X, T, nch, xb, sq, tag="lnp"):
    sch = p.sched
    W = nch * T
    sch.op("vector", lambda e: e.tensor_copy(xb[:, 0:W], X[:, 0:W]),
           reads=[X], writes=[xb], tag=f"{tag}_cast")
    sch.op("scalar", lambda e: e.activation(sq[:, 0:W], X[:, 0:W], AF.Square),
           reads=[X], writes=[sq], tag=f"{tag}_sq")


def emit_ln_main(p, C, T, nch, xb, sq, psA, psB, bc_ps, rows, tag="ln"):
    sch = p.sched
    ND = nch * 128
    for c in range(nch):
        sch.op("tensor", lambda e, c=c: e.matmul(
            psA[0:1, 0:T], C.ones_col_bf[:, 0:1], xb[:, c * T:(c + 1) * T],
            start=(c == 0), stop=(c == nch - 1)),
            reads=[xb, C.ones_col_bf], writes=[psA], tag=f"{tag}_mA{c}")
    for c in range(nch):
        sch.op("tensor", lambda e, c=c: e.matmul(
            psB[0:1, 0:T], C.ones_col_bf[:, 0:1], sq[:, c * T:(c + 1) * T],
            start=(c == 0), stop=(c == nch - 1)),
            reads=[sq, C.ones_col_bf], writes=[psB], tag=f"{tag}_mB{c}")
    sch.op("vector", lambda e: e.tensor_scalar_mul(rows["mu"][0:1, 0:T],
                                                   psA[0:1, 0:T], 1.0 / ND),
           reads=[psA], writes=[rows["mu"]], tag=f"{tag}_mu")
    sch.op("vector", lambda e: e.scalar_tensor_tensor(
        out=rows["var"][0:1, 0:T], in0=rows["mu"][0:1, 0:T], scalar=-1.0,
        in1=rows["mu"][0:1, 0:T], op0=ALU.mult, op1=ALU.mult),
        reads=[rows["mu"]], writes=[rows["var"]], tag=f"{tag}_m2")
    sch.op("vector", lambda e: e.scalar_tensor_tensor(
        out=rows["var"][0:1, 0:T], in0=psB[0:1, 0:T], scalar=1.0 / ND,
        in1=rows["var"][0:1, 0:T], op0=ALU.mult, op1=ALU.add),
        reads=[psB, rows["var"]], writes=[rows["var"]], tag=f"{tag}_var")
    sch.op("scalar", lambda e: e.activation(rows["rstd"][0:1, 0:T],
                                            rows["var"][0:1, 0:T], AF.Ln,
                                            bias=C.eps5[0:1, 0:1]),
           reads=[rows["var"], C.eps5], writes=[rows["rstd"]], tag=f"{tag}_l")
    sch.op("scalar", lambda e: e.activation(rows["rstd"][0:1, 0:T],
                                            rows["rstd"][0:1, 0:T], AF.Exp,
                                            scale=-0.5),
           reads=[rows["rstd"]], writes=[rows["rstd"]], tag=f"{tag}_e")
    sch.op("vector", lambda e: e.scalar_tensor_tensor(
        out=rows["negmurstd"][0:1, 0:T], in0=rows["mu"][0:1, 0:T], scalar=-1.0,
        in1=rows["rstd"][0:1, 0:T], op0=ALU.mult, op1=ALU.mult),
        reads=[rows["mu"], rows["rstd"]], writes=[rows["negmurstd"]], tag=f"{tag}_n")
    sch.op("vector", lambda e: e.tensor_copy(rows["negmurstd_bf"][0:1, 0:T],
                                             rows["negmurstd"][0:1, 0:T]),
           reads=[rows["negmurstd"]], writes=[rows["negmurstd_bf"]], tag=f"{tag}_nb")
    sch.op("tensor", lambda e: e.matmul(bc_ps[:, 0:T], C.ones_row_f32[0:1, :],
                                        rows["rstd"][0:1, 0:T], start=True, stop=True),
           reads=[rows["rstd"], C.ones_row_f32], writes=[bc_ps], tag=f"{tag}_bc")


def emit_ln(p, C, X, T, nch, xb, sq, psA, psB, bc_ps, rows, tag="ln"):
    emit_ln_prep(p, X, T, nch, xb, sq, tag=tag + "p")
    emit_ln_main(p, C, T, nch, xb, sq, psA, psB, bc_ps, rows, tag=tag)


def emit_xtilde(p, X, bc_ps, out_buf, T, nch, tag="xt"):
    for c in range(nch):
        p.sched.op("vector", lambda e, c=c: e.tensor_mul(
            out_buf[:, c * T:(c + 1) * T], X[:, c * T:(c + 1) * T], bc_ps[:, 0:T]),
            reads=[X, bc_ps], writes=[out_buf], tag=f"{tag}{c}")


def emit_mm_group(p, wT_sb, act_bf, T, nk, O_total, psums, evac_fn,
                  rank1s=(), tag="mm"):
    sch = p.sched
    noo = O_total // 128
    n_extra = len(rank1s)
    for oo in range(noo):
        psum = psums[oo % len(psums)]
        for kc in range(nk):
            sch.op("tensor", lambda e, kc=kc, oo=oo, psum=psum: e.matmul(
                psum[:, 0:T],
                wT_sb[:, kc * O_total + oo * 128: kc * O_total + oo * 128 + 128],
                act_bf[:, kc * T:(kc + 1) * T],
                start=(kc == 0), stop=(kc == nk - 1 and n_extra == 0)),
                reads=[wT_sb, act_bf], writes=[psum], tag=f"{tag}o{oo}k{kc}")
        for j, (lhs_fn, rhs_ap, rbufs) in enumerate(rank1s):
            sch.op("tensor", lambda e, oo=oo, psum=psum, lhs_fn=lhs_fn, rhs_ap=rhs_ap, j=j:
                   e.matmul(psum[:, 0:T], lhs_fn(oo), rhs_ap,
                            start=False, stop=(j == n_extra - 1)),
                   reads=list(rbufs), writes=[psum], tag=f"{tag}o{oo}r{j}")
        evac_fn(oo, psum)


def emit_trig(p, U, W, kscr, sin_out, cos_out, tag="tr"):
    sch = p.sched
    sch.op("vector", lambda e: e.tensor_scalar(
        out=kscr[:, 0:W], in0=U[:, 0:W], scalar1=MAGIC, scalar2=MAGIC,
        op0=ALU.add, op1=ALU.subtract),
        reads=[U], writes=[kscr], tag=f"{tag}_k")
    sch.op("vector", lambda e: e.tensor_sub(sin_out[:, 0:W], U[:, 0:W], kscr[:, 0:W]),
           reads=[U, kscr], writes=[sin_out], tag=f"{tag}_r")
    sch.op("vector", lambda e: e.tensor_scalar(
        out=kscr[:, 0:W], in0=sin_out[:, 0:W], scalar1=0.25, scalar2=MAGIC,
        op0=ALU.add, op1=ALU.add),
        reads=[sin_out], writes=[kscr], tag=f"{tag}_r3")
    sch.op("vector", lambda e: e.tensor_scalar(
        out=kscr[:, 0:W], in0=kscr[:, 0:W], scalar1=MAGIC, scalar2=0.25,
        op0=ALU.subtract, op1=ALU.subtract),
        reads=[kscr], writes=[kscr], tag=f"{tag}_kq")
    sch.op("vector", lambda e: e.tensor_sub(cos_out[:, 0:W], sin_out[:, 0:W],
                                            kscr[:, 0:W]),
           reads=[sin_out, kscr], writes=[cos_out], tag=f"{tag}_r2")
    sch.op("scalar", lambda e: e.activation(sin_out[:, 0:W], sin_out[:, 0:W],
                                            AF.Sin, scale=TWOPI),
           reads=[sin_out], writes=[sin_out], tag=f"{tag}_s")
    sch.op("scalar", lambda e: e.activation(cos_out[:, 0:W], cos_out[:, 0:W],
                                            AF.Sin, scale=TWOPI),
           reads=[cos_out], writes=[cos_out], tag=f"{tag}_c")


def emit_wsum(p, C, wb, nk, O_total, name, psw, stack=None):
    sch = p.sched
    row_bf = p.sb([1, O_total], BF16, name + "_bf", stack=stack)
    for og in range((O_total + 511) // 512):
        o0, o1 = og * 512, min((og + 1) * 512, O_total)
        for kc in range(nk):
            sch.op("tensor", lambda e, kc=kc, o0=o0, o1=o1: e.matmul(
                psw[0:1, 0:o1 - o0], C.ones_col_bf[:, 0:1],
                wb[:, kc * O_total + o0: kc * O_total + o1],
                start=(kc == 0), stop=(kc == nk - 1)),
                reads=[wb, C.ones_col_bf], writes=[psw], tag=f"{name}{og}_{kc}")
        sch.op("vector", lambda e, o0=o0, o1=o1: e.tensor_copy(
            row_bf[0:1, o0:o1], psw[0:1, 0:o1 - o0]),
            reads=[psw], writes=[row_bf], tag=f"{name}cp{og}")
    return row_bf



NAMES_L0A = ["L0_attn_W_magT", "L0_attn_b_mag_c", "L0_attn_W_qT",
             "L0_attn_b_q_c", "L0_attn_random_phases_r", "pos_row",
             "L0_attn_W1T", "L0_attn_b1_c", "L0_attn_W2T", "L0_attn_b2_c",
             "L0_attn_ln_g_c", "L0_attn_ln_b_c"]


def emit_l0_attn(p, C, io, T=256, dbg=False):
    sch = p.sched
    p.prefix = "A_"
    xin_b, xout_b = io["xin"], io["xout"]
    x_h, xout_h = xin_b.t, xout_b.t
    wmagT_h = io["L0_attn_W_magT"]
    bmag_h = io["L0_attn_b_mag_c"]
    wqT_h = io["L0_attn_W_qT"]
    bq_h = io["L0_attn_b_q_c"]
    rp_h = io["L0_attn_random_phases_r"]
    pos_h = io["pos_row"]
    w1T_h = io["L0_attn_W1T"]
    b1_h = io["L0_attn_b1_c"]
    w2T_h = io["L0_attn_W2T"]
    b2_h = io["L0_attn_b2_c"]
    lng_h = io["L0_attn_ln_g_c"]
    lnb_h = io["L0_attn_ln_b_c"]
    mk = p.dram_out if dbg else p.dram_tmp
    ctx_h = mk("ctx", [B, 4 * D, S], BF16)
    hbuf_h = mk("l0h", [B, 2 * D, S], BF16)
    ctx_b = Buf(ctx_h, "ctx")
    hb_b = Buf(hbuf_h, "l0h")
    dbgw_h = p.dram_out("dbgw", [128, 32 * D], BF16) if dbg else None
    dbgb_h = p.dram_out("dbgb", [128, 16], F32) if dbg else None

    W = NCH * T
    NPT = S // T

    T1 = 256
    W1_ = NCH * T1
    NPT1 = S // T1
    ph = ExitStack()
    wmag = load_wT(p, wmagT_h, D, D, "wmag", stack=ph)
    wq = load_wT(p, wqT_h, D, D, "wq", stack=ph)
    bmag = load_cols(p, bmag_h, NCH, "bmag", stack=ph)
    bq = load_cols(p, bq_h, NCH, "bq", stack=ph)
    rp = p.sb([1, D], F32, "rp", stack=ph)
    sch.op("sync", lambda e: e.dma_start(out=rp[:], in_=rp_h[:]), writes=[rp],
           is_dma=True, tag="rp_ld")
    _posT = p.sb([1, 512], F32, "posT", stack=ph)
    posTs = [_posT, _posT]
    rps = rp
    sch.op("vector", lambda e: e.tensor_scalar_mul(rp[0:1, :], rp[0:1, :],
                                                   float(0.01 * INV2PI)),
           reads=[rp], writes=[rp], tag="rps")
    bq2 = p.sb([128, NCH], F32, "bq2", stack=ph)
    sch.op("vector", lambda e: e.tensor_scalar_mul(bq2[:], bq[:], INV2PI),
           reads=[bq], writes=[bq2], tag="bq2")

    psA = p.ps([128, 512], "psA", stack=ph)
    psB = p.ps([128, 512], "psB", stack=ph)
    ps_bc = p.ps([128, 512], "ps_bc", stack=ph)
    ps_m0 = p.ps([128, 512], "ps_m0", stack=ph)
    ps_m1 = p.ps([128, 512], "ps_m1", stack=ph)
    ps_u = p.ps([128, 512], "ps_u", stack=ph)
    ps_u2 = p.ps([128, 512], "ps_u2", stack=ph)

    wmagsum_bf = emit_wsum(p, C, wmag, NCH, D, "wmagsum", ps_m0, stack=ph)
    wqsum_bf = emit_wsum(p, C, wq, NCH, D, "wqsum", ps_m0, stack=ph)

    Xs = [p.sb([128, W1_], F32, f"pX{j}", stack=ph) for j in range(2)]
    XBs = [p.sb([128, W1_], BF16, f"pXB{j}", stack=ph) for j in range(2)]
    SQBs = [p.sb([128, W1_], BF16, f"pSQB{j}", stack=ph) for j in range(2)]
    UFMs = [p.sb([128, W1_], F32, f"pUFM{j}", stack=ph) for j in range(2)]
    COSPs = [p.sb([128, W1_], F32, f"pCOS{j}", stack=ph) for j in range(2)]
    SINPs = [p.sb([128, W1_], F32, f"pSIN{j}", stack=ph) for j in range(2)]
    KSCR1 = p.sb([128, W1_], F32, "pKS", stack=ph)
    XLN = p.sb([128, W1_], F32, "pXLN", stack=ph)
    MAGT = p.sb([128, W1_], F32, "pMAGT", stack=ph)
    CQ = p.sb([128, W1_], F32, "pCQ", stack=ph)
    MEMR = p.sb([128, W1_], F32, "pMEMR", stack=ph)
    MEMI = p.sb([128, W1_], F32, "pMEMI", stack=ph)
    ACC = p.sb([128, W1_], F32, "pACC", stack=ph)
    SCR1 = p.sb([128, W1_], F32, "pSCR1", stack=ph)
    SCR2 = p.sb([128, W1_], F32, "pSCR2", stack=ph)
    WGT = ACC
    RAWS = p.sb([128, 512], F32, "pRAWS", stack=ph)
    QOUT = p.sb([128, W1_], BF16, "pQOUT", stack=ph)
    rows = make_rows(p, T1, stack=ph)
    carries = {s_: p.sb([128, NCH * B], F32, "carry_" + s_, stack=ph)
               for s_ in ("r", "i", "a")}
    for s_ in carries:
        sch.op("vector", lambda e, s_=s_: e.memset(carries[s_].ap, 0.0),
               writes=[carries[s_]], tag="car0")

    mu_r1 = [(lambda oo, w=wmagsum_bf: w[0:1, oo * 128:(oo + 1) * 128],
              rows["negmurstd_bf"][0:1, 0:T1], [rows["negmurstd_bf"]])]
    mu_r1q = [(lambda oo, w=wqsum_bf: w[0:1, oo * 128:(oo + 1) * 128],
               rows["negmurstd_bf"][0:1, 0:T1], [rows["negmurstd_bf"]])]

    tiles1 = [(pt, b) for pt in range(NPT1) for b in range(B)]

    def prep1(i):
        pt, b = tiles1[i]
        t0 = pt * T1
        X = Xs[i % 2]
        load_tile(p, X, x_h[b, :, t0:t0 + T1], NCH, T1, tag=f"px{i}",
                  hbuf=xin_b)
        emit_ln_prep(p, X, T1, NCH, XBs[i % 2], SQBs[i % 2], tag=f"plp{i}")
        if b == 0:
            j = pt % 2
            posT, UFM, COSP, SINP, KSCR = (posTs[j], UFMs[j], COSPs[j],
                                           SINPs[j], KSCR1)
            sch.op("sync", lambda e, t0=t0, posT=posT: e.dma_start(
                out=posT[0:1, 0:T1], in_=pos_h[0:1, t0:t0 + T1]),
                writes=[posT], is_dma=True, tag=f"pos{pt}")
            for c in range(NCH):
                sch.op("tensor", lambda e, c=c, posT=posT: e.matmul(
                    ps_u2[:, 0:T1], rps[0:1, c * 128:(c + 1) * 128],
                    posT[0:1, 0:T1], start=True, stop=True),
                    reads=[rps, posT], writes=[ps_u2], tag=f"phi{pt}_{c}")
                sch.op("vector", lambda e, c=c, UFM=UFM: e.tensor_copy(
                    UFM[:, c * T1:(c + 1) * T1], ps_u2[:, 0:T1]),
                    reads=[ps_u2], writes=[UFM], tag=f"phic{pt}_{c}")
            emit_trig(p, UFM, W1_, KSCR, SINP, COSP, tag=f"tr{pt}")

    def main1(i):
        pt, b = tiles1[i]
        t0 = pt * T1
        j = pt % 2
        X, XB, SQB = Xs[i % 2], XBs[i % 2], SQBs[i % 2]
        UFM, COSP, SINP = UFMs[j], COSPs[j], SINPs[j]
        QOUT2 = SQB
        SQT = KSCR1
        UQ = SCR1
        emit_ln_main(p, C, T1, NCH, XB, SQB, psA, psB, ps_bc, rows,
                     tag=f"pln{i}")
        emit_xtilde(p, X, ps_bc, XB, T1, NCH, tag=f"pxt{i}")
        sch.op("tensor", lambda e: e.matmul(
            ps_u[:, 0:T1], C.ones_row_f32[0:1, :], rows["negmurstd"][0:1, 0:T1],
            start=True, stop=True),
            reads=[rows["negmurstd"], C.ones_row_f32], writes=[ps_u],
            tag=f"pnmb{i}")
        emit_xtilde(p, X, ps_bc, XLN, T1, NCH, tag=f"pxl{i}")
        for c in range(NCH):
            sch.op("vector", lambda e, c=c: e.tensor_add(
                XLN[:, c * T1:(c + 1) * T1], XLN[:, c * T1:(c + 1) * T1],
                ps_u[:, 0:T1]),
                reads=[XLN, ps_u], writes=[XLN], tag=f"pxa{i}{c}")

        def evac_mag(oo, psum):
            sch.op("scalar", lambda e, oo=oo, psum=psum: e.activation(
                MAGT[:, oo * T1:(oo + 1) * T1], psum[:, 0:T1], AF.Sigmoid,
                bias=bmag[:, oo:oo + 1]),
                reads=[psum, bmag], writes=[MAGT], tag=f"pmv{i}{oo}")
        emit_mm_group(p, wmag, XB, T1, NCH, D, (ps_m0, ps_m1), evac_mag,
                      rank1s=mu_r1, tag=f"pm{i}")

        def evac_q(oo, psum):
            sch.op("scalar", lambda e, oo=oo, psum=psum: e.activation(
                UQ[:, oo * T1:(oo + 1) * T1], psum[:, 0:T1], AF.Identity,
                scale=INV2PI, bias=bq2[:, oo:oo + 1]),
                reads=[psum, bq2], writes=[UQ], tag=f"pqv{i}{oo}")
        emit_mm_group(p, wq, XB, T1, NCH, D, (ps_m0, ps_m1), evac_q,
                      rank1s=mu_r1q, tag=f"pq{i}")
        sch.op("vector", lambda e, UFM=UFM: e.tensor_add(
            UQ[:, 0:W1_], UQ[:, 0:W1_], UFM[:, 0:W1_]),
            reads=[UQ, UFM], writes=[UQ], tag=f"puq{i}")
        emit_trig(p, UQ, W1_, SCR2, SQT, CQ, tag=f"ptq{i}")

        sch.op("gpsimd", lambda e: e.tensor_mul(WGT[:, 0:W1_], MAGT[:, 0:W1_],
                                                XLN[:, 0:W1_]),
               reads=[MAGT, XLN], writes=[WGT], tag=f"pwg{i}")
        sch.op("gpsimd", lambda e, COSP=COSP: e.tensor_mul(
            SCR2[:, 0:W1_], WGT[:, 0:W1_], COSP[:, 0:W1_]),
            reads=[WGT, COSP], writes=[SCR2], tag=f"pwc{i}")
        sch.op("gpsimd", lambda e, SINP=SINP: e.tensor_mul(
            SCR1[:, 0:W1_], WGT[:, 0:W1_], SINP[:, 0:W1_]),
            reads=[WGT, SINP], writes=[SCR1], tag=f"pws{i}")

        def scans(srcb, dst, key):
            car = carries[key]
            for c in range(NCH):
                cc_ = np.s_[:, c * T1:(c + 1) * T1]
                col = np.s_[:, b * NCH + c: b * NCH + c + 1]
                sch.op("vector", lambda e, cc_=cc_, srcb=srcb: e.tensor_tensor_scan(
                    out=RAWS[:, 0:T1], data0=srcb[cc_], data1=srcb[cc_],
                    initial=0.0, op0=ALU.add, op1=ALU.bypass),
                    reads=[srcb], writes=[RAWS], tag=f"psc{key}{i}{c}")
                sch.op("vector", lambda e, cc_=cc_, col=col, dst=dst: e.tensor_scalar_add(
                    dst[cc_], RAWS[:, 0:T1], car[col]),
                    reads=[RAWS, car], writes=[dst], tag=f"pcb{key}{i}{c}")
                sch.op("vector", lambda e, col=col: e.tensor_add(
                    car[col], car[col], RAWS[:, T1 - 1:T1]),
                    reads=[car, RAWS], writes=[car], tag=f"pcu{key}{i}{c}")
        scans(SCR2, MEMR, "r")
        scans(SCR1, MEMI, "i")
        scans(MAGT, ACC, "a")

        sch.op("scalar", lambda e: e.activation(ACC[:, 0:W1_], ACC[:, 0:W1_],
                                                AF.Ln, scale=5.0,
                                                bias=C.eps8[:, 0:1]),
               reads=[ACC, C.eps8], writes=[ACC], tag=f"pisl{i}")
        sch.op("scalar", lambda e: e.activation(ACC[:, 0:W1_], ACC[:, 0:W1_],
                                                AF.Exp, scale=-0.5,
                                                bias=C.ln5[:, 0:1]),
               reads=[ACC, C.ln5], writes=[ACC], tag=f"pise{i}")
        sch.op("vector", lambda e: e.tensor_mul(MEMR[:, 0:W1_], MEMR[:, 0:W1_],
                                                ACC[:, 0:W1_]),
               reads=[MEMR, ACC], writes=[MEMR], tag=f"pmr{i}")
        sch.op("vector", lambda e: e.tensor_mul(MEMI[:, 0:W1_], MEMI[:, 0:W1_],
                                                ACC[:, 0:W1_]),
               reads=[MEMI, ACC], writes=[MEMI], tag=f"pmi{i}")

        def quarter_store(q, buf, tag):
            src3 = bass.AP(buf.t, 0, [[W1_, 128], [T1, NCH], [1, T1]])
            dst = ctx_h[b, q * D:(q + 1) * D, t0:t0 + T1].rearrange(
                "(c p) t -> p c t", p=128)
            sch.op("scalar", lambda e, dst=dst, src3=src3: e.dma_start(
                out=dst, in_=src3),
                reads=[buf], writes=[ctx_b], is_dma=True, tag=tag)

        sch.op("gpsimd", lambda e: e.tensor_mul(SCR1[:, 0:W1_], MEMR[:, 0:W1_],
                                                CQ[:, 0:W1_]),
               reads=[MEMR, CQ], writes=[SCR1], tag=f"pt1{i}")
        sch.op("vector", lambda e: e.tensor_mul(SCR2[:, 0:W1_], MEMI[:, 0:W1_],
                                                SQT[:, 0:W1_]),
               reads=[MEMI, SQT], writes=[SCR2], tag=f"pt2{i}")
        sch.op("vector", lambda e: e.tensor_add(QOUT[:, 0:W1_], SCR1[:, 0:W1_],
                                                SCR2[:, 0:W1_]),
               reads=[SCR1, SCR2], writes=[QOUT], tag=f"prr{i}")
        quarter_store(2, QOUT, f"pstr{i}")
        sch.op("gpsimd", lambda e: e.tensor_mul(SCR1[:, 0:W1_], MEMI[:, 0:W1_],
                                                CQ[:, 0:W1_]),
               reads=[MEMI, CQ], writes=[SCR1], tag=f"pt3{i}")
        sch.op("vector", lambda e: e.tensor_mul(SCR2[:, 0:W1_], MEMR[:, 0:W1_],
                                                SQT[:, 0:W1_]),
               reads=[MEMR, SQT], writes=[SCR2], tag=f"pt4{i}")
        sch.op("vector", lambda e: e.tensor_sub(QOUT2[:, 0:W1_], SCR1[:, 0:W1_],
                                                SCR2[:, 0:W1_]),
               reads=[SCR1, SCR2], writes=[QOUT2], tag=f"pri{i}")
        quarter_store(3, QOUT2, f"psti{i}")
        sch.op("gpsimd", lambda e, COSP=COSP: e.tensor_mul(
            QOUT[:, 0:W1_], XLN[:, 0:W1_], COSP[:, 0:W1_]),
            reads=[XLN, COSP], writes=[QOUT], tag=f"pq1{i}")
        quarter_store(0, QOUT, f"pst1{i}")
        sch.op("gpsimd", lambda e, SINP=SINP: e.tensor_mul(
            QOUT2[:, 0:W1_], XLN[:, 0:W1_], SINP[:, 0:W1_]),
            reads=[XLN, SINP], writes=[QOUT2], tag=f"pq2{i}")
        quarter_store(1, QOUT2, f"pst2{i}")

    prep1(0)
    for i in range(len(tiles1)):
        if i + 1 < len(tiles1):
            prep1(i + 1)
        main1(i)

    sch.flush()
    ph.close()
    sch.barrier()

    NF = 32
    TA = 256
    TT = 2 * TA
    NPTA = S // TA
    for half in range(2):
        ph2 = ExitStack()
        OH = D
        w1 = p.sb([128, NF * OH], BF16, f"w1h{half}", stack=ph2)
        for fc in range(NF):
            sch.op("gpsimd", lambda e, fc=fc: e.dma_start(
                out=w1[:, fc * OH:(fc + 1) * OH],
                in_=w1T_h[fc * 128:(fc + 1) * 128, half * OH:(half + 1) * OH]),
                writes=[w1], is_dma=True, tag=f"w1l{half}_{fc}")
        b1c = load_cols(p, b1_h, 16, f"b1c{half}", stack=ph2)
        lng = load_cols(p, lng_h, NF, f"lng{half}", stack=ph2)
        lnb = load_cols(p, lnb_h, NF, f"lnb{half}", stack=ph2)
        lnb_bf = p.sb([128, NF], BF16, f"lnbbf{half}", stack=ph2)
        sch.op("vector", lambda e: e.tensor_copy(lnb_bf[:], lnb[:]),
               reads=[lnb], writes=[lnb_bf], tag=f"lnbb{half}")
        psA2 = p.ps([128, 512], f"psA2{half}", stack=ph2)
        psB2 = p.ps([128, 512], f"psB2{half}", stack=ph2)
        ps_bc2 = p.ps([128, 512], f"psbc2{half}", stack=ph2)
        ps_n0 = p.ps([128, 512], f"psn0{half}", stack=ph2)
        ps_n1 = p.ps([128, 512], f"psn1{half}", stack=ph2)
        ps_u2 = p.ps([128, 512], f"psu2{half}", stack=ph2)

        w1b_row = p.sb([1, OH], F32, f"w1br{half}", stack=ph2)
        for og in range(OH // 512):
            for fc in range(NF):
                sch.op("tensor", lambda e, og=og, fc=fc: e.matmul(
                    ps_n0[0:1, 0:512], lnb_bf[:, fc:fc + 1],
                    w1[:, fc * OH + og * 512: fc * OH + (og + 1) * 512],
                    start=(fc == 0), stop=(fc == NF - 1)),
                    reads=[w1, lnb_bf], writes=[ps_n0], tag=f"w1b{half}{og}_{fc}")
            sch.op("vector", lambda e, og=og: e.tensor_copy(
                w1b_row[0:1, og * 512:(og + 1) * 512], ps_n0[0:1, 0:512]),
                reads=[ps_n0], writes=[w1b_row], tag=f"w1bc{half}{og}")
        biasH = p.sb([128, 8], F32, f"biasH{half}", stack=ph2)
        for j in range(8):
            sch.op("tensor", lambda e, j=j: e.matmul(
                ps_u2[:, j:j + 1], w1b_row[0:1, j * 128:(j + 1) * 128],
                C.ones_row_f32[0:1, 0:1], start=True, stop=True),
                reads=[w1b_row, C.ones_row_f32], writes=[ps_u2], tag=f"w1bt{half}{j}")
        sch.op("vector", lambda e, half=half: e.tensor_add(
            biasH[:], ps_u2[:, 0:8], b1c[:, half * 8:(half + 1) * 8]),
            reads=[ps_u2, b1c], writes=[biasH], tag=f"biasH{half}")

        for fc in range(NF):
            sch.op("scalar", lambda e, fc=fc: e.activation(
                w1[:, fc * OH:(fc + 1) * OH], w1[:, fc * OH:(fc + 1) * OH],
                AF.Identity, scale=lng[:, fc:fc + 1]),
                reads=[w1, lng], writes=[w1], tag=f"w1g{half}{fc}")
        w1gsum_bf = emit_wsum(p, C, w1, NF, OH, f"w1gs{half}", ps_n0, stack=ph2)

        CTXs = [p.sb([128, NF * TT], BF16, f"CTX{half}_{j}", stack=ph2)
                for j in range(2)]
        CSQB = p.sb([128, NF * TT], BF16, f"CSQB{half}", stack=ph2)
        HOUT = p.sb([128, 8 * TT], BF16, f"HOUT{half}", stack=ph2)
        SCRas = [p.sb([128, 512], F32, f"SCRa{half}{j}", stack=ph2)
                 for j in range(2)]
        rstd_sb = p.sb([128, 512], F32, f"rstdsb{half}", stack=ph2)
        rows2 = make_rows(p, TT, stack=ph2)
        rows2["negmu_bf"] = p.sb([1, TT], BF16, f"negmubf{half}", stack=ph2)
        mu_r1w1 = [(lambda oo, w=w1gsum_bf: w[0:1, oo * 128:(oo + 1) * 128],
                    rows2["negmu_bf"][0:1, 0:TT], [rows2["negmu_bf"]])]

        def prep2a(i, half=half, CTXs=CTXs):
            t0 = i * TA
            CTX = CTXs[i % 2]
            for b in range(B):
                srcap = ctx_h[b, :, t0:t0 + TA].rearrange("(c p) t -> p c t", p=128)
                sch.op("sync", lambda e, srcap=srcap, CTX=CTX, b=b: e.dma_start(
                    out=bass.AP(CTX.t, b * TA, [[NF * TT, 128], [TT, NF], [1, TA]]),
                    in_=srcap),
                    writes=[CTX], reads=[ctx_b], is_dma=True,
                    tag=f"ldc{half}_{i}_{b}")

        def main2a(i, half=half, CTXs=CTXs, CSQB=CSQB, HOUT=HOUT, SCRas=SCRas,
                   rstd_sb=rstd_sb, rows2=rows2, w1=w1, biasH=biasH,
                   mu_r1w1=mu_r1w1, OH=OH):
            t0 = i * TA
            CTX = CTXs[i % 2]
            sch.op("vector", lambda e, CTX=CTX: e.tensor_mul(
                CSQB[:, 0:NF * TT], CTX[:, 0:NF * TT], CTX[:, 0:NF * TT]),
                reads=[CTX], writes=[CSQB], tag=f"cs{half}{i}")
            for c in range(NF):
                sch.op("tensor", lambda e, c=c, CTX=CTX: e.matmul(
                    psA2[0:1, 0:TT], C.ones_col_bf[:, 0:1],
                    CTX[:, c * TT:(c + 1) * TT],
                    start=(c == 0), stop=(c == NF - 1)),
                    reads=[CTX, C.ones_col_bf], writes=[psA2],
                    tag=f"cA{half}{i}_{c}")
            for c in range(NF):
                sch.op("tensor", lambda e, c=c: e.matmul(
                    psB2[0:1, 0:TT], C.ones_col_bf[:, 0:1],
                    CSQB[:, c * TT:(c + 1) * TT],
                    start=(c == 0), stop=(c == NF - 1)),
                    reads=[CSQB, C.ones_col_bf], writes=[psB2],
                    tag=f"cB{half}{i}_{c}")
            ND2 = NF * 128
            sch.op("vector", lambda e: e.tensor_scalar_mul(
                rows2["mu"][0:1, 0:TT], psA2[0:1, 0:TT], 1.0 / ND2),
                reads=[psA2], writes=[rows2["mu"]], tag=f"cmu{half}{i}")
            sch.op("vector", lambda e: e.scalar_tensor_tensor(
                out=rows2["var"][0:1, 0:TT], in0=rows2["mu"][0:1, 0:TT],
                scalar=-1.0, in1=rows2["mu"][0:1, 0:TT],
                op0=ALU.mult, op1=ALU.mult),
                reads=[rows2["mu"]], writes=[rows2["var"]], tag=f"cm2{half}{i}")
            sch.op("vector", lambda e: e.scalar_tensor_tensor(
                out=rows2["var"][0:1, 0:TT], in0=psB2[0:1, 0:TT],
                scalar=1.0 / ND2, in1=rows2["var"][0:1, 0:TT],
                op0=ALU.mult, op1=ALU.add),
                reads=[psB2, rows2["var"]], writes=[rows2["var"]],
                tag=f"cv{half}{i}")
            sch.op("scalar", lambda e: e.activation(
                rows2["rstd"][0:1, 0:TT], rows2["var"][0:1, 0:TT], AF.Ln,
                bias=C.eps5[0:1, 0:1]),
                reads=[rows2["var"], C.eps5], writes=[rows2["rstd"]],
                tag=f"cl{half}{i}")
            sch.op("scalar", lambda e: e.activation(
                rows2["rstd"][0:1, 0:TT], rows2["rstd"][0:1, 0:TT], AF.Exp,
                scale=-0.5),
                reads=[rows2["rstd"]], writes=[rows2["rstd"]], tag=f"ce{half}{i}")
            sch.op("vector", lambda e: e.tensor_scalar_mul(
                rows2["negmu_bf"][0:1, 0:TT], rows2["mu"][0:1, 0:TT], -1.0),
                reads=[rows2["mu"]], writes=[rows2["negmu_bf"]],
                tag=f"cnb{half}{i}")
            sch.op("tensor", lambda e: e.matmul(
                ps_bc2[:, 0:TT], C.ones_row_f32[0:1, :], rows2["rstd"][0:1, 0:TT],
                start=True, stop=True),
                reads=[rows2["rstd"], C.ones_row_f32], writes=[ps_bc2],
                tag=f"cbc{half}{i}")
            sch.op("vector", lambda e: e.tensor_copy(
                rstd_sb[:, 0:TT], ps_bc2[:, 0:TT]),
                reads=[ps_bc2], writes=[rstd_sb], tag=f"cbs{half}{i}")

            def evac_h(oo, psum):
                SCRa = SCRas[oo % 2]
                sch.op("vector", lambda e, oo=oo, psum=psum, SCRa=SCRa: e.tensor_mul(
                    SCRa[:, 0:TT], psum[:, 0:TT], rstd_sb[:, 0:TT]),
                    reads=[psum, rstd_sb], writes=[SCRa], tag=f"hm{half}{i}o{oo}")
                sch.op("scalar", lambda e, oo=oo, SCRa=SCRa: e.activation(
                    HOUT[:, oo * TT:(oo + 1) * TT], SCRa[:, 0:TT], AF.Gelu,
                    bias=biasH[:, oo:oo + 1]),
                    reads=[SCRa, biasH], writes=[HOUT], tag=f"hg{half}{i}o{oo}")
            emit_mm_group(p, w1, CTX, TT, NF, OH, (ps_n0, ps_n1), evac_h,
                          rank1s=mu_r1w1, tag=f"h{half}{i}")
            for b in range(B):
                dst = hbuf_h[b, half * OH:(half + 1) * OH, t0:t0 + TA].rearrange(
                    "(c p) t -> p c t", p=128)
                sch.op("scalar", lambda e, dst=dst, b=b: e.dma_start(
                    out=dst,
                    in_=bass.AP(HOUT.t, b * TA, [[8 * TT, 128], [TT, 8], [1, TA]])),
                    reads=[HOUT], writes=[hb_b], is_dma=True,
                    tag=f"sh{half}_{i}_{b}")

        prep2a(0)
        for i in range(NPTA):
            if i + 1 < NPTA:
                prep2a(i + 1)
            main2a(i)
        sch.flush()
        ph2.close()
        sch.barrier()

    ph3 = ExitStack()
    w2 = load_wT(p, w2T_h, 2 * D, D, "w2", stack=ph3)
    b2c = load_cols(p, b2_h, NCH, "b2c", stack=ph3)
    psA3 = p.ps([128, 512], "psA3", stack=ph3)
    psB3 = p.ps([128, 512], "psB3", stack=ph3)
    ps_bc3 = p.ps([128, 512], "psbc3", stack=ph3)
    ps_n3 = p.ps([128, 512], "psn3", stack=ph3)
    ps_n4 = p.ps([128, 512], "psn4", stack=ph3)
    ps_u3 = p.ps([128, 512], "psu3", stack=ph3)
    HBs = [p.sb([128, 16 * TT], BF16, f"HB{j}", stack=ph3) for j in range(2)]
    X2s = [p.sb([128, NCH * TT], F32, f"X2{j}", stack=ph3) for j in range(2)]
    XB2s = [p.sb([128, NCH * TT], BF16, f"XB2{j}", stack=ph3) for j in range(2)]
    SQB2s = [p.sb([128, NCH * TT], BF16, f"SQB2{j}", stack=ph3) for j in range(2)]
    XLN2 = p.sb([128, NCH * TT], F32, "XLN2", stack=ph3)
    rows3 = make_rows(p, TT, stack=ph3)

    def prep3(i):
        t0 = i * TA
        HB = HBs[i % 2]
        for b in range(B):
            srcap = hbuf_h[b, :, t0:t0 + TA].rearrange("(c p) t -> p c t", p=128)
            sch.op("sync", lambda e, srcap=srcap, HB=HB, b=b: e.dma_start(
                out=bass.AP(HB.t, b * TA, [[16 * TT, 128], [TT, 16], [1, TA]]),
                in_=srcap),
                writes=[HB], reads=[hb_b], is_dma=True, tag=f"lh{i}_{b}")
            srcx = x_h[b, :, t0:t0 + TA].rearrange("(c p) t -> p c t", p=128)
            X2 = X2s[i % 2]
            sch.op("sync", lambda e, srcx=srcx, X2=X2, b=b: e.dma_start(
                out=bass.AP(X2.t, b * TA, [[NCH * TT, 128], [TT, NCH], [1, TA]]),
                in_=srcx),
                writes=[X2], reads=[xin_b], is_dma=True, tag=f"lx{i}_{b}")
        emit_ln_prep(p, X2s[i % 2], TT, NCH, XB2s[i % 2], SQB2s[i % 2],
                     tag=f"l3p{i}")

    def main3(i):
        t0 = i * TA
        HB, X2, XB2, SQB2 = HBs[i % 2], X2s[i % 2], XB2s[i % 2], SQB2s[i % 2]
        emit_ln_main(p, C, TT, NCH, XB2, SQB2, psA3, psB3, ps_bc3, rows3,
                     tag=f"l3{i}")
        sch.op("tensor", lambda e: e.matmul(
            ps_u3[:, 0:TT], C.ones_row_f32[0:1, :],
            rows3["negmurstd"][0:1, 0:TT], start=True, stop=True),
            reads=[rows3["negmurstd"], C.ones_row_f32], writes=[ps_u3],
            tag=f"n3{i}")
        emit_xtilde(p, X2, ps_bc3, XLN2, TT, NCH, tag=f"x3{i}")
        for c in range(NCH):
            sch.op("vector", lambda e, c=c: e.tensor_add(
                XLN2[:, c * TT:(c + 1) * TT], XLN2[:, c * TT:(c + 1) * TT],
                ps_u3[:, 0:TT]),
                reads=[XLN2, ps_u3], writes=[XLN2], tag=f"x3a{i}{c}")

        def evac_o(oo, psum):
            sch.op("vector", lambda e, oo=oo, psum=psum: e.scalar_tensor_tensor(
                out=X2[:, oo * TT:(oo + 1) * TT], in0=psum[:, 0:TT],
                scalar=b2c[:, oo:oo + 1], in1=X2[:, oo * TT:(oo + 1) * TT],
                op0=ALU.add, op1=ALU.add),
                reads=[psum, b2c, X2], writes=[X2], tag=f"o1{i}{oo}")
            sch.op("vector", lambda e, oo=oo: e.tensor_add(
                X2[:, oo * TT:(oo + 1) * TT], X2[:, oo * TT:(oo + 1) * TT],
                XLN2[:, oo * TT:(oo + 1) * TT]),
                reads=[X2, XLN2], writes=[X2], tag=f"o2{i}{oo}")
        emit_mm_group(p, w2, HB, TT, 16, D, (ps_n3, ps_n4), evac_o,
                      tag=f"o{i}")
        for b in range(B):
            dstx = xout_h[b, :, t0:t0 + TA].rearrange("(c p) t -> p c t", p=128)
            sch.op("scalar", lambda e, dstx=dstx, X2=X2, b=b: e.dma_start(
                out=dstx,
                in_=bass.AP(X2.t, b * TA, [[NCH * TT, 128], [TT, NCH], [1, TA]])),
                reads=[X2], writes=[xout_b], is_dma=True, tag=f"so{i}_{b}")

    prep3(0)
    for i in range(NPTA):
        if i + 1 < NPTA:
            prep3(i + 1)
        main3(i)

    sch.flush()
    ph3.close()
    sch.barrier()


NAMES_L0F = ["L0_ffn_W_rT", "L0_ffn_W_kT", "L0_ffn_W_vT",
             "L0_ffn_mu_r_c", "L0_ffn_mu_k_c"]


def emit_l0_ffn(p, C, io, T=512):
    sch = p.sched
    p.prefix = "F_"
    xin_b, xout_b = io["xin"], io["xout"]
    x_h, xout_h = xin_b.t, xout_b.t
    wrT_h = io["L0_ffn_W_rT"]
    wkT_h = io["L0_ffn_W_kT"]
    wvT_h = io["L0_ffn_W_vT"]
    mur_h = io["L0_ffn_mu_r_c"]
    muk_h = io["L0_ffn_mu_k_c"]
    ph = ExitStack()
    _old_sb, _old_ps = p.sb, p.ps
    p.sb = lambda *a, stack=None, **k: _old_sb(*a, stack=(stack or ph), **k)
    p.ps = lambda *a, stack=None, **k: _old_ps(*a, stack=(stack or ph), **k)

    W = NCH * T
    NPT = S // T

    wr = load_wT(p, wrT_h, D, D, "wr")
    wk = load_wT(p, wkT_h, D, D, "wk")
    wv = load_wT(p, wvT_h, D, D, "wv")
    mur = load_cols(p, mur_h, NCH, "mur")
    muk = load_cols(p, muk_h, NCH, "muk")

    psA = p.ps([128, 512], "psA")
    psB = p.ps([128, 512], "psB")
    ps_bc = p.ps([128, 512], "ps_bc")
    ps_m0 = p.ps([128, 512], "ps_m0")
    ps_m1 = p.ps([128, 512], "ps_m1")
    ps_u = p.ps([128, 512], "ps_u")

    Xs = [p.sb([128, W], F32, f"X{j}") for j in range(2)]
    XBs = [p.sb([128, W], BF16, f"XB{j}") for j in range(2)]
    SQBs = [p.sb([128, W], BF16, f"SQB{j}") for j in range(2)]
    XLN = p.sb([128, W], F32, "XLN")
    XPREV = p.sb([128, W], F32, "XPREV")
    DD = p.sb([128, W], F32, "DD")
    XR = p.sb([128, W], BF16, "XR")
    XK = p.sb([128, W], BF16, "XK")
    R = p.sb([128, W], F32, "R")
    KA = XR
    OUT = DD
    rows = make_rows(p, T)
    xcar = p.sb([128, NCH * B], F32, "xcar")
    sch.op("vector", lambda e: e.memset(xcar.ap, 0.0), writes=[xcar], tag="xc0")

    tiles = [(pt, b) for pt in range(NPT) for b in range(B)]

    def prep(i):
        pt, b = tiles[i]
        t0 = pt * T
        X, XB, SQB = Xs[i % 2], XBs[i % 2], SQBs[i % 2]
        load_tile(p, X, x_h[b, :, t0:t0 + T], NCH, T, tag=f"x{pt}_{b}",
                  hbuf=xin_b)
        emit_ln_prep(p, X, T, NCH, XB, SQB, tag=f"lp{pt}_{b}")

    def main(i):
        pt, b = tiles[i]
        t0 = pt * T
        X, XB, SQB = Xs[i % 2], XBs[i % 2], SQBs[i % 2]
        emit_ln_main(p, C, T, NCH, XB, SQB, psA, psB, ps_bc, rows,
                     tag=f"ln{pt}_{b}")
        sch.op("tensor", lambda e: e.matmul(
            ps_u[:, 0:T], C.ones_row_f32[0:1, :], rows["negmurstd"][0:1, 0:T],
            start=True, stop=True),
            reads=[rows["negmurstd"], C.ones_row_f32], writes=[ps_u],
            tag=f"nmb{pt}_{b}")
        emit_xtilde(p, X, ps_bc, XLN, T, NCH, tag=f"xl{pt}_{b}")
        for c in range(NCH):
            sch.op("vector", lambda e, c=c: e.tensor_add(
                XLN[:, c * T:(c + 1) * T], XLN[:, c * T:(c + 1) * T],
                ps_u[:, 0:T]),
                reads=[XLN, ps_u], writes=[XLN], tag=f"xa{pt}_{b}{c}")
        sch.op("vector", lambda e, b=b: e.tensor_copy(
            bass.AP(XPREV.t, 0, [[W, 128], [T, NCH]]),
            xcar[:, b * NCH:(b + 1) * NCH]),
            reads=[xcar], writes=[XPREV], tag=f"shc{pt}_{b}")
        for c in range(NCH):
            sch.op("vector", lambda e, c=c: e.tensor_copy(
                XPREV[:, c * T + 1:(c + 1) * T], XLN[:, c * T:(c + 1) * T - 1]),
                reads=[XLN], writes=[XPREV], tag=f"shf{pt}_{b}{c}")
        sch.op("vector", lambda e, b=b: e.tensor_copy(
            xcar[:, b * NCH:(b + 1) * NCH],
            bass.AP(XLN.t, T - 1, [[W, 128], [T, NCH]])),
            reads=[XLN], writes=[xcar], tag=f"cu{pt}_{b}")
        sch.op("vector", lambda e: e.tensor_sub(DD[:, 0:W], XLN[:, 0:W],
                                                XPREV[:, 0:W]),
               reads=[XLN, XPREV], writes=[DD], tag=f"d{pt}_{b}")
        for c in range(NCH):
            sch.op("vector", lambda e, c=c: e.scalar_tensor_tensor(
                out=XR[:, c * T:(c + 1) * T], in0=DD[:, c * T:(c + 1) * T],
                scalar=mur[:, c:c + 1], in1=XPREV[:, c * T:(c + 1) * T],
                op0=ALU.mult, op1=ALU.add),
                reads=[DD, mur, XPREV], writes=[XR], tag=f"xr{pt}_{b}{c}")
        for c in range(NCH):
            sch.op("vector", lambda e, c=c: e.scalar_tensor_tensor(
                out=XK[:, c * T:(c + 1) * T], in0=DD[:, c * T:(c + 1) * T],
                scalar=muk[:, c:c + 1], in1=XPREV[:, c * T:(c + 1) * T],
                op0=ALU.mult, op1=ALU.add),
                reads=[DD, muk, XPREV], writes=[XK], tag=f"xk{pt}_{b}{c}")

        def evac_r(oo, psum):
            sch.op("scalar", lambda e, oo=oo, psum=psum: e.activation(
                R[:, oo * T:(oo + 1) * T], psum[:, 0:T], AF.Sigmoid),
                reads=[psum], writes=[R], tag=f"rv{pt}_{b}{oo}")
        emit_mm_group(p, wr, XR, T, NCH, D, (ps_m0, ps_m1), evac_r,
                      tag=f"r{pt}_{b}")

        def evac_k(oo, psum):
            sch.op("scalar", lambda e, oo=oo, psum=psum: e.activation(
                XLN[:, oo * T:(oo + 1) * T], psum[:, 0:T], AF.Relu),
                reads=[psum], writes=[XLN], tag=f"kv{pt}_{b}{oo}")
        emit_mm_group(p, wk, XK, T, NCH, D, (ps_m0, ps_m1), evac_k,
                      tag=f"k{pt}_{b}")
        sch.op("scalar", lambda e: e.activation(KA[:, 0:W], XLN[:, 0:W],
                                                AF.Square),
               reads=[XLN], writes=[KA], tag=f"ka{pt}_{b}")

        def evac_v(oo, psum):
            sch.op("vector", lambda e, oo=oo, psum=psum: e.tensor_mul(
                OUT[:, oo * T:(oo + 1) * T], psum[:, 0:T],
                R[:, oo * T:(oo + 1) * T]),
                reads=[psum, R], writes=[OUT], tag=f"vv{pt}_{b}{oo}")
            sch.op("vector", lambda e, oo=oo: e.tensor_add(
                OUT[:, oo * T:(oo + 1) * T], OUT[:, oo * T:(oo + 1) * T],
                X[:, oo * T:(oo + 1) * T]),
                reads=[OUT, X], writes=[OUT], tag=f"vr{pt}_{b}{oo}")
        emit_mm_group(p, wv, KA, T, NCH, D, (ps_m0, ps_m1), evac_v,
                      tag=f"v{pt}_{b}")
        store_tile(p, xout_h[b, :, t0:t0 + T], OUT, NCH, T, tag=f"so{pt}_{b}",
                   hbuf=xout_b)

    prep(0)
    for i in range(len(tiles)):
        if i + 1 < len(tiles):
            prep(i + 1)
        main(i)

    sch.flush()
    p.sb, p.ps = _old_sb, _old_ps
    ph.close()
    sch.barrier()


NAMES_L1A = ["L1_attn_W_bT", "L1_attn_W_cT", "L1_attn_log_a_c",
             "L1_attn_gamma_c", "L1_attn_beta_c", "pos_row"]


def emit_l1_attn(p, C, io, T=512):
    sch = p.sched
    p.prefix = "G_"
    xin_b, xout_b = io["xin"], io["xout"]
    x_h, xout_h = xin_b.t, xout_b.t
    wbT_h = io["L1_attn_W_bT"]
    wcT_h = io["L1_attn_W_cT"]
    loga_h = io["L1_attn_log_a_c"]
    gam_h = io["L1_attn_gamma_c"]
    bet_h = io["L1_attn_beta_c"]
    pos_h = io["pos_row"]
    ph = ExitStack()
    _old_sb, _old_ps = p.sb, p.ps
    p.sb = lambda *a, stack=None, **k: _old_sb(*a, stack=(stack or ph), **k)
    p.ps = lambda *a, stack=None, **k: _old_ps(*a, stack=(stack or ph), **k)

    W = NCH * T
    NPT = S // T

    wb = load_wT(p, wbT_h, D, D, "wb")
    wc = load_wT(p, wcT_h, D, D, "wc")
    loga = load_cols(p, loga_h, NCH, "loga")
    gam = load_cols(p, gam_h, NCH, "gam")
    bet = load_cols(p, bet_h, NCH, "bet")
    pos = p.sb([1, 516], F32, "posr")
    sch.op("sync", lambda e: e.dma_start(out=pos[0:1, 0:512],
                                         in_=pos_h[0:1, 1:513]),
           writes=[pos], is_dma=True, tag="pos_ld")

    psA = p.ps([128, 512], "psA")
    psB = p.ps([128, 512], "psB")
    ps_bc = p.ps([128, 512], "ps_bc")
    ps_m0 = p.ps([128, 512], "ps_m0")
    ps_m1 = p.ps([128, 512], "ps_m1")
    ps_u = p.ps([128, 512], "ps_u")

    wbsum_bf = emit_wsum(p, C, wb, NCH, D, "wbsum", ps_m0)

    acol = p.sb([128, NCH], F32, "acol")
    sch.op("scalar", lambda e: e.activation(acol[:], loga[:], AF.Exp),
           reads=[loga], writes=[acol], tag="a_exp")
    aTc = p.sb([128, NCH], F32, "aTc")
    sch.op("scalar", lambda e: e.activation(aTc[:], loga[:], AF.Exp,
                                            scale=float(T)),
           reads=[loga], writes=[aTc], tag="aT_exp")
    ABC = p.sb([128, W], F32, "ABC")
    APOW = p.sb([128, W], F32, "APOW")
    RAWS = p.sb([128, 512], F32, "RAWS")
    ZW = RAWS
    sch.op("vector", lambda e: e.memset(ZW.ap, 0.0), writes=[ZW], tag="zw")
    for c in range(NCH):
        sch.op("scalar", lambda e, c=c: e.activation(
            ABC[:, c * T:(c + 1) * T], ZW[:, 0:T], AF.Identity,
            bias=acol[:, c:c + 1]),
            reads=[ZW, acol], writes=[ABC], tag=f"abc{c}")
    for c in range(NCH):
        sch.op("tensor", lambda e: e.matmul(
            ps_u[:, 0:T], C.ones_row_f32[0:1, :], pos[0:1, 0:T],
            start=True, stop=True),
            reads=[pos, C.ones_row_f32], writes=[ps_u], tag=f"apm{c}")
        sch.op("scalar", lambda e, c=c: e.activation(
            APOW[:, c * T:(c + 1) * T], ps_u[:, 0:T], AF.Exp,
            scale=loga[:, c:c + 1]),
            reads=[ps_u, loga], writes=[APOW], tag=f"ape{c}")

    Xs = [p.sb([128, W], F32, f"X{j}") for j in range(2)]
    XBs = [p.sb([128, W], BF16, f"XB{j}") for j in range(2)]
    SQBs = [p.sb([128, W], BF16, f"SQB{j}") for j in range(2)]
    STB = p.sb([128, W], BF16, "STB")
    Y = p.sb([128, W], F32, "Y")
    SCRt = p.sb([128, W], F32, "SCRt")
    UBS = SCRt
    rstd_sb = p.sb([128, 512], F32, "grstd")
    YSQ = STB
    rows = make_rows(p, T)
    rows["negmu_bf"] = p.sb([1, T], BF16, "gnegmu")
    rows_y = make_rows(p, T)
    scar = p.sb([128, NCH * B], F32, "scar")
    sch.op("vector", lambda e: e.memset(scar.ap, 0.0), writes=[scar], tag="sc0")

    mu_r1 = [(lambda oo, w=wbsum_bf: w[0:1, oo * 128:(oo + 1) * 128],
              rows["negmu_bf"][0:1, 0:T], [rows["negmu_bf"]])]

    tiles = [(pt, b) for pt in range(NPT) for b in range(B)]

    def prep(i):
        pt, b = tiles[i]
        t0 = pt * T
        X, XB, SQB = Xs[i % 2], XBs[i % 2], SQBs[i % 2]
        load_tile(p, X, x_h[b, :, t0:t0 + T], NCH, T, tag=f"x{pt}_{b}",
                  hbuf=xin_b)
        emit_ln_prep(p, X, T, NCH, XB, SQB, tag=f"lp{pt}_{b}")

    def main(i):
        pt, b = tiles[i]
        t0 = pt * T
        X, XB, SQB = Xs[i % 2], XBs[i % 2], SQBs[i % 2]
        YB = SQB
        emit_ln_main(p, C, T, NCH, XB, SQB, psA, psB, ps_bc, rows,
                     tag=f"ln{pt}_{b}")
        sch.op("vector", lambda e: e.tensor_scalar_mul(
            rows["negmu_bf"][0:1, 0:T], rows["mu"][0:1, 0:T], -1.0),
            reads=[rows["mu"]], writes=[rows["negmu_bf"]], tag=f"gnm{pt}_{b}")
        sch.op("vector", lambda e: e.tensor_copy(rstd_sb[:, 0:T], ps_bc[:, 0:T]),
               reads=[ps_bc], writes=[rstd_sb], tag=f"grs{pt}_{b}")

        def evac_ub(oo, psum):
            c = oo
            cc_ = np.s_[:, c * T:(c + 1) * T]
            col = np.s_[:, b * NCH + c: b * NCH + c + 1]
            sch.op("vector", lambda e, psum=psum: e.tensor_mul(
                UBS[:, 0:T], psum[:, 0:T], rstd_sb[:, 0:T]),
                reads=[psum, rstd_sb], writes=[UBS], tag=f"ubm{pt}_{b}{c}")
            sch.op("vector", lambda e, cc_=cc_: e.tensor_tensor_scan(
                out=RAWS[:, 0:T], data0=ABC[cc_], data1=UBS[:, 0:T],
                initial=0.0, op0=ALU.mult, op1=ALU.add),
                reads=[ABC, UBS], writes=[RAWS], tag=f"em{pt}_{b}{c}")
            sch.op("vector", lambda e, cc_=cc_, col=col: e.scalar_tensor_tensor(
                out=STB[cc_], in0=APOW[cc_], scalar=scar[col],
                in1=RAWS[:, 0:T], op0=ALU.mult, op1=ALU.add),
                reads=[APOW, scar, RAWS], writes=[STB], tag=f"ec{pt}_{b}{c}")
            sch.op("vector", lambda e, c=c, col=col: e.scalar_tensor_tensor(
                out=scar[col], in0=scar[col], scalar=aTc[:, c:c + 1],
                in1=RAWS[:, T - 1:T], op0=ALU.mult, op1=ALU.add),
                reads=[scar, aTc, RAWS], writes=[scar], tag=f"eu{pt}_{b}{c}")
        emit_mm_group(p, wb, XB, T, NCH, D, (ps_m0, ps_m1), evac_ub,
                      rank1s=mu_r1, tag=f"ub{pt}_{b}")

        def evac_y(oo, psum):
            sch.op("vector", lambda e, oo=oo, psum=psum: e.tensor_copy(
                Y[:, oo * T:(oo + 1) * T], psum[:, 0:T]),
                reads=[psum], writes=[Y], tag=f"yv{pt}_{b}{oo}")
        emit_mm_group(p, wc, STB, T, NCH, D, (ps_m0, ps_m1), evac_y,
                      tag=f"y{pt}_{b}")

        emit_ln(p, C, Y, T, NCH, YB, YSQ, psA, psB, ps_bc, rows_y,
                tag=f"lny{pt}_{b}")
        sch.op("tensor", lambda e: e.matmul(
            ps_u[:, 0:T], C.ones_row_f32[0:1, :], rows_y["mu"][0:1, 0:T],
            start=True, stop=True),
            reads=[rows_y["mu"], C.ones_row_f32], writes=[ps_u], tag=f"myb{pt}_{b}")
        for c in range(NCH):
            sch.op("vector", lambda e, c=c: e.tensor_sub(
                SCRt[:, c * T:(c + 1) * T], Y[:, c * T:(c + 1) * T],
                ps_u[:, 0:T]),
                reads=[Y, ps_u], writes=[SCRt], tag=f"ys{pt}_{b}{c}")
        for c in range(NCH):
            sch.op("vector", lambda e, c=c: e.tensor_mul(
                SCRt[:, c * T:(c + 1) * T], SCRt[:, c * T:(c + 1) * T],
                ps_bc[:, 0:T]),
                reads=[SCRt, ps_bc], writes=[SCRt], tag=f"yr{pt}_{b}{c}")
        for c in range(NCH):
            sch.op("scalar", lambda e, c=c: e.activation(
                SCRt[:, c * T:(c + 1) * T], SCRt[:, c * T:(c + 1) * T],
                AF.Identity, scale=gam[:, c:c + 1], bias=bet[:, c:c + 1]),
                reads=[SCRt, gam, bet], writes=[SCRt], tag=f"yg{pt}_{b}{c}")
        sch.op("vector", lambda e: e.tensor_add(SCRt[:, 0:W], SCRt[:, 0:W],
                                                X[:, 0:W]),
               reads=[SCRt, X], writes=[SCRt], tag=f"or{pt}_{b}")
        store_tile(p, xout_h[b, :, t0:t0 + T], SCRt, NCH, T, tag=f"so{pt}_{b}",
                   hbuf=xout_b)

    prep(0)
    for i in range(len(tiles)):
        if i + 1 < len(tiles):
            prep(i + 1)
        main(i)

    sch.flush()
    p.sb, p.ps = _old_sb, _old_ps
    ph.close()
    sch.barrier()


NAMES_L1F = ["L1_ffn_w1T", "L1_ffn_w2T", "L1_ffn_h1T", "L1_ffn_h2T",
             "L1_ffn_w3T", "L1_ffn_w1_s_c", "L1_ffn_w2_s_c", "L1_ffn_h1_s_c",
             "L1_ffn_h2_s_c", "L1_ffn_w3_s_c"]


def emit_l1_ffn(p, C, io, T=512):
    sch = p.sched
    p.prefix = "H_"
    xin_b, xout_b = io["xin"], io["xout"]
    x_h, xout_h = xin_b.t, xout_b.t
    w1T_h = io["L1_ffn_w1T"]
    w2T_h = io["L1_ffn_w2T"]
    h1T_h = io["L1_ffn_h1T"]
    h2T_h = io["L1_ffn_h2T"]
    w3T_h = io["L1_ffn_w3T"]
    s1_h = io["L1_ffn_w1_s_c"]
    s2_h = io["L1_ffn_w2_s_c"]
    h1s_h = io["L1_ffn_h1_s_c"]
    h2s_h = io["L1_ffn_h2_s_c"]
    w3s_h = io["L1_ffn_w3_s_c"]
    hb_h = p.dram_tmp("hbuf", [B, 4 * D, S], BF16)
    g1_h = p.dram_tmp("g1buf", [B, 4 * D, S], BF16)
    g2_h = p.dram_tmp("g2buf", [B, 4 * D, S], BF16)
    hb_b = Buf(hb_h, "hbuf"); g1_b = Buf(g1_h, "g1buf"); g2_b = Buf(g2_h, "g2buf")
    buf_of = {id(hb_h): hb_b, id(g1_h): g1_b, id(g2_h): g2_b}

    W = NCH * T
    W4 = 32 * T
    NPT = S // T
    TA = 256
    WA = NCH * TA
    WA4 = 32 * TA
    NPTA = S // TA

    TA = 256
    TT = 2 * TA
    WA = NCH * TT
    NPTA = S // TA
    for ah in range(2):
        ph = ExitStack()
        OHA = 2 * D
        w1 = p.sb([128, NCH * OHA], BF16, f"w1h{ah}", stack=ph)
        w2 = p.sb([128, NCH * OHA], BF16, f"w2h{ah}", stack=ph)
        for kc in range(NCH):
            sch.op("gpsimd", lambda e, kc=kc: e.dma_start(
                out=w1[:, kc * OHA:(kc + 1) * OHA],
                in_=w1T_h[kc * 128:(kc + 1) * 128, ah * OHA:(ah + 1) * OHA]),
                writes=[w1], is_dma=True, tag=f"aw1l{ah}_{kc}")
            sch.op("gpsimd", lambda e, kc=kc: e.dma_start(
                out=w2[:, kc * OHA:(kc + 1) * OHA],
                in_=w2T_h[kc * 128:(kc + 1) * 128, ah * OHA:(ah + 1) * OHA]),
                writes=[w2], is_dma=True, tag=f"aw2l{ah}_{kc}")
        s1 = load_cols(p, s1_h, 32, f"s1_{ah}", stack=ph)
        s2 = load_cols(p, s2_h, 32, f"s2_{ah}", stack=ph)
        if ah == 0:
            psA = p.ps([128, 512], "psA")
            psB = p.ps([128, 512], "psB")
            ps_bc = p.ps([128, 512], "ps_bc")
            ps_m0 = p.ps([128, 512], "ps_m0")
            ps_m1 = p.ps([128, 512], "ps_m1")
            ps_m2 = p.ps([128, 512], "ps_m2")
            ps_m3 = p.ps([128, 512], "ps_m3")
        w1sum_bf = emit_wsum(p, C, w1, NCH, OHA, f"w1sum{ah}", ps_m0, stack=ph)
        w2sum_bf = emit_wsum(p, C, w2, NCH, OHA, f"w2sum{ah}", ps_m0, stack=ph)

        Xs = [p.sb([128, WA], F32, f"aX{ah}{j}", stack=ph) for j in range(2)]
        XBs = [p.sb([128, WA], BF16, f"aXB{ah}{j}", stack=ph) for j in range(2)]
        SQBs = [p.sb([128, WA], BF16, f"aSQB{ah}{j}", stack=ph) for j in range(2)]
        SILs = [p.sb([128, 512], F32, f"aSIL{ah}{j}", stack=ph) for j in range(2)]
        T2s = [p.sb([128, 512], F32, f"aT2{ah}{j}", stack=ph) for j in range(2)]
        HOUT = p.sb([128, 16 * TT], BF16, f"aH{ah}", stack=ph)
        rows = make_rows(p, TT, stack=ph)
        rows["negmu_bf"] = p.sb([1, TT], BF16, f"anegmu{ah}", stack=ph)
        rstd_sb = p.sb([128, 512], F32, f"arstd{ah}", stack=ph)

        def prepA(i, ah=ah, Xs=Xs, XBs=XBs, SQBs=SQBs):
            t0 = i * TA
            X = Xs[i % 2]
            for b in range(B):
                srcap = x_h[b, :, t0:t0 + TA].rearrange("(c p) t -> p c t", p=128)
                sch.op("sync", lambda e, srcap=srcap, X=X, b=b: e.dma_start(
                    out=bass.AP(X.t, b * TA, [[WA, 128], [TT, NCH], [1, TA]]),
                    in_=srcap),
                    writes=[X], reads=[xin_b], is_dma=True, tag=f"ax{ah}_{i}_{b}")
            emit_ln_prep(p, X, TT, NCH, XBs[i % 2], SQBs[i % 2], tag=f"alp{ah}_{i}")

        def mainA(i, ah=ah, Xs=Xs, XBs=XBs, SQBs=SQBs, SILs=SILs, T2s=T2s,
                  HOUT=HOUT, rows=rows, w1=w1, w2=w2, s1=s1, s2=s2,
                  w1sum_bf=w1sum_bf, w2sum_bf=w2sum_bf, OHA=OHA):
            t0 = i * TA
            X, XB, SQB = Xs[i % 2], XBs[i % 2], SQBs[i % 2]
            emit_ln_main(p, C, TT, NCH, XB, SQB, psA, psB, ps_bc, rows,
                         tag=f"aln{ah}_{i}")
            sch.op("vector", lambda e: e.tensor_scalar_mul(
                rows["negmu_bf"][0:1, 0:TT], rows["mu"][0:1, 0:TT], -1.0),
                reads=[rows["mu"]], writes=[rows["negmu_bf"]], tag=f"anm{ah}_{i}")
            sch.op("vector", lambda e: e.tensor_copy(
                rstd_sb[:, 0:TT], ps_bc[:, 0:TT]),
                reads=[ps_bc], writes=[rstd_sb], tag=f"ars{ah}_{i}")
            for oi in range(16):
                oo = ah * 16 + oi
                pa = (ps_m0, ps_m2)[oi % 2]
                pb = (ps_m1, ps_m3)[oi % 2]
                for kc in range(NCH):
                    sch.op("tensor", lambda e, kc=kc, oi=oi, pa=pa, XB=XB: e.matmul(
                        pa[:, 0:TT],
                        w1[:, kc * OHA + oi * 128: kc * OHA + oi * 128 + 128],
                        XB[:, kc * TT:(kc + 1) * TT],
                        start=(kc == 0), stop=False),
                        reads=[w1, XB], writes=[pa], tag=f"am1{ah}_{i}o{oi}k{kc}")
                sch.op("tensor", lambda e, oi=oi, pa=pa: e.matmul(
                    pa[:, 0:TT], w1sum_bf[0:1, oi * 128:(oi + 1) * 128],
                    rows["negmu_bf"][0:1, 0:TT], start=False, stop=True),
                    reads=[w1sum_bf, rows["negmu_bf"]], writes=[pa],
                    tag=f"am1r{ah}_{i}o{oi}")
                for kc in range(NCH):
                    sch.op("tensor", lambda e, kc=kc, oi=oi, pb=pb, XB=XB: e.matmul(
                        pb[:, 0:TT],
                        w2[:, kc * OHA + oi * 128: kc * OHA + oi * 128 + 128],
                        XB[:, kc * TT:(kc + 1) * TT],
                        start=(kc == 0), stop=False),
                        reads=[w2, XB], writes=[pb], tag=f"am2{ah}_{i}o{oi}k{kc}")
                sch.op("tensor", lambda e, oi=oi, pb=pb: e.matmul(
                    pb[:, 0:TT], w2sum_bf[0:1, oi * 128:(oi + 1) * 128],
                    rows["negmu_bf"][0:1, 0:TT], start=False, stop=True),
                    reads=[w2sum_bf, rows["negmu_bf"]], writes=[pb],
                    tag=f"am2r{ah}_{i}o{oi}")
                SIL, T2 = SILs[oi % 2], T2s[oi % 2]
                sch.op("vector", lambda e, pa=pa, SIL=SIL: e.tensor_mul(
                    SIL[:, 0:TT], pa[:, 0:TT], rstd_sb[:, 0:TT]),
                    reads=[pa, rstd_sb], writes=[SIL], tag=f"ay1{ah}_{i}o{oi}")
                sch.op("scalar", lambda e, oo=oo, SIL=SIL: e.activation(
                    SIL[:, 0:TT], SIL[:, 0:TT], AF.Silu, scale=s1[:, oo:oo + 1]),
                    reads=[SIL, s1], writes=[SIL], tag=f"asl{ah}_{i}o{oo}")
                sch.op("vector", lambda e, pb=pb, T2=T2: e.tensor_mul(
                    T2[:, 0:TT], pb[:, 0:TT], rstd_sb[:, 0:TT]),
                    reads=[pb, rstd_sb], writes=[T2], tag=f"ay2{ah}_{i}o{oi}")
                sch.op("vector", lambda e, oi=oi, oo=oo, SIL=SIL, T2=T2: e.scalar_tensor_tensor(
                    out=HOUT[:, oi * TT:(oi + 1) * TT], in0=T2[:, 0:TT],
                    scalar=s2[:, oo:oo + 1], in1=SIL[:, 0:TT],
                    op0=ALU.mult, op1=ALU.mult),
                    reads=[SIL, T2, s2], writes=[HOUT], tag=f"ah{ah}_{i}o{oi}")
            for b in range(B):
                dst = hb_h[b, ah * OHA:(ah + 1) * OHA, t0:t0 + TA].rearrange(
                    "(c p) t -> p c t", p=128)
                sch.op("scalar", lambda e, dst=dst, b=b: e.dma_start(
                    out=dst,
                    in_=bass.AP(HOUT.t, b * TA, [[16 * TT, 128], [TT, 16], [1, TA]])),
                    reads=[HOUT], writes=[hb_b], is_dma=True,
                    tag=f"ash{ah}_{i}_{b}")

        prepA(0)
        for i in range(NPTA):
            if i + 1 < NPTA:
                prepA(i + 1)
            mainA(i)
        sch.flush()
        ph.close()
        sch.barrier()

    def gelu_phase(win_h, sc_h, src_h, dst_h, name):
        OH = D
        src_b = buf_of[id(src_h)]
        dst_b = buf_of[id(dst_h)]
        ph = ExitStack()
        wts = [p.sb([128, 32 * OH], BF16, f"{name}_w{j}", stack=ph)
               for j in range(2)]
        scc = load_cols(p, sc_h, 32, f"{name}_s", stack=ph)
        INs = [p.sb([128, W4], BF16, f"{name}_in{j}", stack=ph)
               for j in range(2)]
        OUTB = p.sb([128, 8 * T], BF16, f"{name}_out", stack=ph)
        tiles = [(pt, b) for pt in range(NPT) for b in range(B)]

        def load_w(qr):
            wt = wts[qr % 2]
            for kc in range(32):
                sch.op("gpsimd", lambda e, kc=kc, qr=qr, wt=wt: e.dma_start(
                    out=wt[:, kc * OH:(kc + 1) * OH],
                    in_=win_h[kc * 128:(kc + 1) * 128,
                              qr * OH:(qr + 1) * OH]),
                    writes=[wt], is_dma=True, tag=f"{name}w{qr}_{kc}")

        def prep(qr, i):
            pt, b = tiles[i]
            IN = INs[i % 2]
            t0 = pt * T
            srcap = src_h[b, :, t0:t0 + T].rearrange("(c p) t -> p c t", p=128)
            sch.op("sync", lambda e, srcap=srcap, IN=IN: e.dma_start(
                out=bass.AP(IN.t, 0, [[W4, 128], [T, 32], [1, T]]), in_=srcap),
                writes=[IN], reads=[src_b], is_dma=True,
                tag=f"{name}l{qr}_{i}")

        def main(qr, i):
            pt, b = tiles[i]
            IN = INs[i % 2]
            wt = wts[qr % 2]
            t0 = pt * T
            for oi in range(8):
                oo = qr * 8 + oi
                psum = (ps_m0, ps_m1, ps_m2, ps_m3)[oi % 4]
                for kc in range(32):
                    sch.op("tensor", lambda e, kc=kc, oi=oi, psum=psum, IN=IN,
                           wt=wt: e.matmul(
                        psum[:, 0:T],
                        wt[:, kc * OH + oi * 128: kc * OH + oi * 128 + 128],
                        IN[:, kc * T:(kc + 1) * T],
                        start=(kc == 0), stop=(kc == 31)),
                        reads=[wt, IN], writes=[psum],
                        tag=f"{name}m{qr}_{i}o{oi}k{kc}")
                sch.op("scalar", lambda e, oi=oi, oo=oo, psum=psum: e.activation(
                    OUTB[:, oi * T:(oi + 1) * T], psum[:, 0:T], AF.Gelu,
                    scale=scc[:, oo:oo + 1]),
                    reads=[psum, scc], writes=[OUTB],
                    tag=f"{name}g{qr}_{i}o{oi}")
            dst = dst_h[b, qr * OH:(qr + 1) * OH, t0:t0 + T].rearrange(
                "(c p) t -> p c t", p=128)
            sch.op("scalar", lambda e, dst=dst: e.dma_start(
                out=dst, in_=bass.AP(OUTB.t, 0, [[8 * T, 128], [T, 8], [1, T]])),
                reads=[OUTB], writes=[dst_b], is_dma=True,
                tag=f"{name}s{qr}_{i}")

        load_w(0)
        for qr in range(4):
            prep(qr, 0)
            for i in range(len(tiles)):
                if i == 0 and qr + 1 < 4:
                    load_w(qr + 1)
                if i + 1 < len(tiles):
                    prep(qr, i + 1)
                main(qr, i)
        sch.flush()
        ph.close()
        sch.barrier()

    gelu_phase(h1T_h, h1s_h, hb_h, g1_h, "pB")
    gelu_phase(h2T_h, h2s_h, g1_h, g2_h, "pC")

    ph = ExitStack()
    w3 = load_wT(p, w3T_h, 4 * D, D, "w3", stack=ph)
    s3 = load_cols(p, w3s_h, NCH, "s3", stack=ph)
    INs = [p.sb([128, W4], BF16, f"d_in{j}", stack=ph) for j in range(2)]
    XDs = [p.sb([128, W], F32, f"XD{j}", stack=ph) for j in range(2)]
    OUTD = p.sb([128, W], F32, "OUTD", stack=ph)
    SCRd = p.sb([128, 512], F32, "SCRd", stack=ph)
    tilesD = [(pt, b) for pt in range(NPT) for b in range(B)]

    def prepD(i):
        pt, b = tilesD[i]
        t0 = pt * T
        IN = INs[i % 2]
        srcap = g2_h[b, :, t0:t0 + T].rearrange("(c p) t -> p c t", p=128)
        sch.op("sync", lambda e, srcap=srcap, IN=IN: e.dma_start(
            out=bass.AP(IN.t, 0, [[W4, 128], [T, 32], [1, T]]), in_=srcap),
            writes=[IN], reads=[g2_b], is_dma=True, tag=f"dl{pt}_{b}")
        load_tile(p, XDs[i % 2], x_h[b, :, t0:t0 + T], NCH, T,
                  tag=f"dx{pt}_{b}", hbuf=xin_b)

    def mainD(i):
        pt, b = tilesD[i]
        t0 = pt * T
        IN, XD = INs[i % 2], XDs[i % 2]
        for oo in range(NCH):
            psum = (ps_m0, ps_m1)[oo % 2]
            for kc in range(32):
                sch.op("tensor", lambda e, kc=kc, oo=oo, psum=psum, IN=IN: e.matmul(
                    psum[:, 0:T],
                    w3[:, kc * D + oo * 128: kc * D + oo * 128 + 128],
                    IN[:, kc * T:(kc + 1) * T],
                    start=(kc == 0), stop=(kc == 31)),
                    reads=[w3, IN], writes=[psum], tag=f"dm{pt}_{b}{oo}k{kc}")
            sch.op("scalar", lambda e, oo=oo, psum=psum: e.activation(
                SCRd[:, 0:T], psum[:, 0:T], AF.Identity, scale=s3[:, oo:oo + 1]),
                reads=[psum, s3], writes=[SCRd], tag=f"dsc{pt}_{b}{oo}")
            sch.op("vector", lambda e, oo=oo, XD=XD: e.tensor_add(
                OUTD[:, oo * T:(oo + 1) * T], SCRd[:, 0:T],
                XD[:, oo * T:(oo + 1) * T]),
                reads=[SCRd, XD], writes=[OUTD], tag=f"dr{pt}_{b}{oo}")
        store_tile(p, xout_h[b, :, t0:t0 + T], OUTD, NCH, T, tag=f"ds{pt}_{b}",
                   hbuf=xout_b)

    prepD(0)
    for i in range(len(tilesD)):
        if i + 1 < len(tilesD):
            prepD(i + 1)
        mainD(i)
    sch.flush()
    ph.close()
    sch.barrier()



def cols(v, n):
    return np.ascontiguousarray(v.reshape(n, 128).T)


def prep_weights(inputs):
    f = np.float32
    w = {}
    w["L0_attn_W_magT"] = np.ascontiguousarray(inputs["L0_attn_W_mag"].T)
    w["L0_attn_b_mag_c"] = cols(inputs["L0_attn_b_mag"], NCH)
    w["L0_attn_W_qT"] = np.ascontiguousarray(inputs["L0_attn_W_q"].T)
    w["L0_attn_b_q_c"] = cols(inputs["L0_attn_b_q"], NCH)
    w["L0_attn_random_phases_r"] = inputs["L0_attn_random_phases"].reshape(1, D)
    w["pos_row"] = np.arange(S, dtype=f).reshape(1, S)
    w["L0_attn_W1T"] = np.ascontiguousarray(inputs["L0_attn_W1"].T)
    w["L0_attn_b1_c"] = cols(inputs["L0_attn_b1"], 16)
    w["L0_attn_W2T"] = np.ascontiguousarray(inputs["L0_attn_W2"].T)
    w["L0_attn_b2_c"] = cols(inputs["L0_attn_b2"], NCH)
    w["L0_attn_ln_g_c"] = cols(inputs["L0_attn_ln_g"], 32)
    w["L0_attn_ln_b_c"] = cols(inputs["L0_attn_ln_b"], 32)
    w["L0_ffn_W_rT"] = np.ascontiguousarray(inputs["L0_ffn_W_r"].T)
    w["L0_ffn_W_kT"] = np.ascontiguousarray(inputs["L0_ffn_W_k"].T)
    w["L0_ffn_W_vT"] = np.ascontiguousarray(inputs["L0_ffn_W_v"].T)
    w["L0_ffn_mu_r_c"] = cols(inputs["L0_ffn_mu_r"], NCH)
    w["L0_ffn_mu_k_c"] = cols(inputs["L0_ffn_mu_k"], NCH)
    w["L1_attn_W_bT"] = np.ascontiguousarray(inputs["L1_attn_W_b"].T)
    w["L1_attn_W_cT"] = np.ascontiguousarray(inputs["L1_attn_W_c"].T)
    w["L1_attn_log_a_c"] = cols(inputs["L1_attn_log_a"], NCH)
    w["L1_attn_gamma_c"] = cols(inputs["L1_attn_gamma"], NCH)
    w["L1_attn_beta_c"] = cols(inputs["L1_attn_beta"], NCH)
    w["L1_ffn_w1T"] = np.ascontiguousarray(inputs["L1_ffn_w1_w"].T)
    w["L1_ffn_w2T"] = np.ascontiguousarray(inputs["L1_ffn_w2_w"].T)
    w["L1_ffn_h1T"] = np.ascontiguousarray(inputs["L1_ffn_h1_w"].T)
    w["L1_ffn_h2T"] = np.ascontiguousarray(inputs["L1_ffn_h2_w"].T)
    w["L1_ffn_w3T"] = np.ascontiguousarray(inputs["L1_ffn_w3_w"].T)
    w["L1_ffn_w1_s_c"] = cols(inputs["L1_ffn_w1_s"], 32)
    w["L1_ffn_w2_s_c"] = cols(inputs["L1_ffn_w2_s"], 32)
    w["L1_ffn_h1_s_c"] = cols(inputs["L1_ffn_h1_s"], 32)
    w["L1_ffn_h2_s_c"] = cols(inputs["L1_ffn_h2_s"], 32)
    w["L1_ffn_w3_s_c"] = cols(inputs["L1_ffn_w3_s"], NCH)
    return w


WEIGHT_SHAPES = {
    "L0_attn_W_magT": ([D, D], F32), "L0_attn_b_mag_c": ([128, NCH], F32),
    "L0_attn_W_qT": ([D, D], F32), "L0_attn_b_q_c": ([128, NCH], F32),
    "L0_attn_random_phases_r": ([1, D], F32), "pos_row": ([1, S], F32),
    "L0_attn_W1T": ([4 * D, 2 * D], F32), "L0_attn_b1_c": ([128, 16], F32),
    "L0_attn_W2T": ([2 * D, D], F32), "L0_attn_b2_c": ([128, NCH], F32),
    "L0_attn_ln_g_c": ([128, 32], F32), "L0_attn_ln_b_c": ([128, 32], F32),
    "L0_ffn_W_rT": ([D, D], F32), "L0_ffn_W_kT": ([D, D], F32),
    "L0_ffn_W_vT": ([D, D], F32), "L0_ffn_mu_r_c": ([128, NCH], F32),
    "L0_ffn_mu_k_c": ([128, NCH], F32),
    "L1_attn_W_bT": ([D, D], F32), "L1_attn_W_cT": ([D, D], F32),
    "L1_attn_log_a_c": ([128, NCH], F32), "L1_attn_gamma_c": ([128, NCH], F32),
    "L1_attn_beta_c": ([128, NCH], F32),
    "L1_ffn_w1T": ([D, 4 * D], F32), "L1_ffn_w2T": ([D, 4 * D], F32),
    "L1_ffn_h1T": ([4 * D, 4 * D], F32), "L1_ffn_h2T": ([4 * D, 4 * D], F32),
    "L1_ffn_w3T": ([4 * D, D], F32),
    "L1_ffn_w1_s_c": ([128, 32], F32), "L1_ffn_w2_s_c": ([128, 32], F32),
    "L1_ffn_h1_s_c": ([128, 32], F32), "L1_ffn_h2_s_c": ([128, 32], F32),
    "L1_ffn_w3_s_c": ([128, NCH], F32),
}

EMITTERS = {
    "l0a": (emit_l0_attn, NAMES_L0A),
    "l0f": (emit_l0_ffn, NAMES_L0F),
    "l1a": (emit_l1_attn, NAMES_L1A),
    "l1f": (emit_l1_ffn, NAMES_L1F),
}
BLOCK_ORDER = ["l0a", "l0f", "l1a", "l1f"]


def build_program(blocks, dbg=False):
    p = Prog()
    C = Consts(p, 512)
    names = []
    for bl in blocks:
        for n in EMITTERS[bl][1]:
            if n not in names:
                names.append(n)
    io = {}
    for n in names:
        sh, dt = WEIGHT_SHAPES[n]
        io[n] = p.dram_in(n, sh, dt)
    xin = Buf(p.dram_in("xin", [B, D, S]), "xin")
    xout = Buf(p.dram_out("xout", [B, D, S]), "xout")
    cur = xin
    for i, bl in enumerate(blocks):
        nxt = xout if i == len(blocks) - 1 else Buf(
            p.dram_tmp(f"xmid{i}", [B, D, S]), f"xmid{i}")
        bio = dict(io)
        bio["xin"] = cur
        bio["xout"] = nxt
        kw = {"dbg": dbg} if bl == "l0a" and dbg else {}
        EMITTERS[bl][0](p, C, bio, **kw)
        cur = nxt
    nc = p.finish()
    return nc, ["xin"] + names


def build_l0_attn(T=256, dbg=False):
    return build_program(["l0a"], dbg=dbg)


def build_l0_ffn(T=512):
    return build_program(["l0f"])


def build_l1_attn(T=512):
    return build_program(["l1a"])


def build_l1_ffn(T=512):
    return build_program(["l1f"])


def run_blocks(blocks, x_fm_shards, w):
    nc, names = build_program(blocks)
    in_maps = []
    for i in range(NCORES):
        m = {"xin": x_fm_shards[i]}
        for n in names:
            if n != "xin":
                m[n] = w[n]
        in_maps.append(m)
    res = run_bass_kernel_spmd(nc, in_maps, core_ids=list(range(NCORES)))
    return [res.results[i]["xout"] for i in range(NCORES)]


FUSED = True


def kernel(**inputs):
    w = prep_weights(inputs)
    x = np.asarray(inputs["L0_attn_x"], dtype=np.float32)
    shards = [np.ascontiguousarray(x[2 * i:2 * i + 2].transpose(0, 2, 1))
              for i in range(NCORES)]
    if FUSED:
        shards = run_blocks(BLOCK_ORDER, shards, w)
    else:
        for bl in BLOCK_ORDER:
            shards = [np.asarray(s, dtype=np.float32) for s in shards]
            shards = run_blocks([bl], shards, w)
    out = np.empty((16, S, D), dtype=np.float32)
    for i in range(NCORES):
        out[2 * i:2 * i + 2] = np.asarray(shards[i]).transpose(0, 2, 1)
    return out
